# version 10
# baseline (speedup 1.0000x reference)
import numpy as np
import ml_dtypes
import concourse.bass as bass
import concourse.mybir as mybir
from concourse.bass_utils import run_bass_kernel_spmd

F32 = mybir.dt.float32
BF16 = mybir.dt.bfloat16
AF = mybir.ActivationFunctionType
ALU = mybir.AluOpType
AXX = mybir.AxisListType.X

P = 128
T = 2048
C = 1024
NCCH = 8
NT = 16
NCORES = 8
TC = 256
NCH = T // TC
TPC = TC // P
NE = 8
EPS = 1e-5
INV_TOTAL = 1.0 / 32768.0

ENGINES = ("sync", "tensor", "vector", "scalar", "gpsimd")

_DEBUG_TAPS = []
_DEBUG_CANARY = False
_DEBUG_DUMPS = False


class _Op:
    __slots__ = ("engine", "emit", "reads", "writes", "inc", "waits", "tag",
                 "base")

    def __init__(self, engine, emit, reads, writes, inc, tag):
        self.engine = engine
        self.emit = emit
        self.reads = reads
        self.writes = writes
        self.inc = inc
        self.waits = None
        self.tag = tag
        self.base = 0


class Plan:
    def __init__(self):
        self.ops = []
        self.barrier_tok = None

    def add(self, engine, emit, reads=(), writes=(), inc=1, tag=""):
        r = list(reads)
        if self.barrier_tok is not None:
            r.append(self.barrier_tok)
        self.ops.append(_Op(engine, emit, r, list(writes), inc, tag))

    def barrier(self, tok):
        bt = self.barrier_tok
        for en in ENGINES:
            self.add(en, lambda eng, sem: eng.nop(), reads=([bt] if bt else []),
                     writes=[tok], tag="barrier")
        self.barrier_tok = tok

    def schedule(self):
        last_writer = {}
        readers = {}
        cum = {en: 0 for en in ENGINES}
        for op in self.ops:
            deps = {}
            for r in op.reads:
                w = last_writer.get(r)
                if w is not None:
                    deps[w[0]] = max(deps.get(w[0], 0), w[1])
            for w_ in op.writes:
                w = last_writer.get(w_)
                if w is not None:
                    deps[w[0]] = max(deps.get(w[0], 0), w[1])
                for en, cval in readers.get(w_, ()):
                    deps[en] = max(deps.get(en, 0), cval)
            op.base = cum[op.engine]
            cum[op.engine] += op.inc
            my = (op.engine, cum[op.engine])
            for r in op.reads:
                readers.setdefault(r, []).append(my)
            for w_ in op.writes:
                last_writer[w_] = my
                readers[w_] = []
            op.waits = deps
        self.final = dict(cum)

    def emit_engine(self, engine_name, eng, sems):
        watermark = {en: 0 for en in ENGINES}
        for op in self.ops:
            if op.engine != engine_name:
                continue
            for en in ENGINES:
                c = op.waits.get(en, 0)
                if c > watermark[en]:
                    eng.wait_ge(sems[en], c)
                    watermark[en] = c
            if getattr(op.emit, "_wants_base", False):
                ins = op.emit(eng, sems[engine_name], op.base)
            else:
                ins = op.emit(eng, sems[engine_name])
            if ins is not None:
                ins.then_inc(sems[engine_name], op.inc)


def build_nc():
    nc = bass.Bass("TRN2", target_bir_lowering=False)
    pl = Plan()

    xin = nc.dram_tensor("x", [T, C], F32, kind="ExternalInput").ap()
    wT = {n: nc.dram_tensor(n, [C, C], F32, kind="ExternalInput").ap()
          for n in ("wkT", "wvT", "wrT", "woT")}
    ewT = nc.dram_tensor("ewT", [NE, C, C], F32, kind="ExternalInput").ap()
    gwT = nc.dram_tensor("gwT", [C, NE], F32, kind="ExternalInput").ap()
    vecs = nc.dram_tensor("vecs", [5, P, NCCH], F32, kind="ExternalInput").ap()
    ebF = nc.dram_tensor("ebF", [P, NCCH, NE], F32, kind="ExternalInput").ap()
    gbB = nc.dram_tensor("gbB", [P, NE], F32, kind="ExternalInput").ap()
    ccol = nc.dram_tensor("ccol", [P, 8], F32, kind="ExternalInput").ap()
    id8 = nc.dram_tensor("id8", [8, 8], F32, kind="ExternalInput").ap()
    ones_row = nc.dram_tensor("ones_row", [1, P], F32, kind="ExternalInput").ap()
    zconst = nc.dram_tensor("zconst", [P, 16], F32, kind="ExternalInput").ap()
    ones_tc = nc.dram_tensor("ones_tc", [P, TC + 1], F32, kind="ExternalInput").ap()
    zbf = nc.dram_tensor("zbf", [P, 8], BF16, kind="ExternalInput").ap()
    out = nc.dram_tensor("out", [T, C], F32, kind="ExternalOutput").ap()

    if _DEBUG_DUMPS:
        ekd = nc.dram_tensor("tap_ekd", [NCH, P, NCCH, TC + 1], BF16,
                             kind="ExternalOutput")
        rwd = nc.dram_tensor("tap_rwd", [NCH, P, NCCH, TC], BF16,
                             kind="ExternalOutput")
        psd = nc.dram_tensor("tap_psd", [NCH, P, NCCH, TC + 1], F32,
                             kind="ExternalOutput")
        vfd = nc.dram_tensor("tap_vfd", [NCH, P, NCCH, TC], BF16,
                             kind="ExternalOutput")
    ln1h = nc.dram_tensor("ln1h", [NCCH, T, P], BF16)
    ln2h = nc.dram_tensor("ln2h", [NCCH, T, P], BF16)
    acch = nc.dram_tensor("acch", [NT, C, P], BF16)
    cc_in = nc.dram_tensor("cc_in", [8, 1], F32)
    cc_out = nc.dram_tensor("cc_out", [8, 1], F32, addr_space="Shared")

    xr_r = xin.rearrange("(n p) c -> p n c", p=P)
    out_r = out.rearrange("(n p) c -> p n c", p=P)
    ln1h_r = ln1h.rearrange("j (n p) q -> p n j q", p=P)
    ln2h_r = ln2h.rearrange("j (n p) q -> p n j q", p=P)
    acch_r = acch.rearrange("i (j p) q -> p j i q", p=P)

    S = nc.alloc_sbuf_tensor
    vec_sb = S("vec_sb", [P, 5, NCCH], F32)
    eb_sb = S("eb_sb", [P, NCCH, NE], F32)
    gb_sb = S("gb_sb", [P, NE], F32)
    cc = S("cc", [P, 8], F32)
    id8_sb = S("id8_sb", [8, 8], F32)
    ones_sb = S("ones_sb", [1, P], F32)
    a_sb = S("a_sb", [P, NCCH], F32)
    eu_sb = S("eu_sb", [P, NCCH], F32)
    bu_sb = S("bu_sb", [P, NCCH], F32)
    tmp8 = S("tmp8", [P, NCCH], F32)
    m_run = S("m_run", [P, NCCH], F32)
    m_new = S("m_new", [P, NCCH], F32)
    m_d = S("m_d", [P, NCCH], F32)
    m_r = S("m_r", [P, NCCH, 1], F32)
    negmp = S("negmp", [P, NCCH], F32)
    mloc = S("mloc", [P, NCCH], F32)
    p_car = S("p_car", [P, NCCH, 1], F32)
    q_car = S("q_car", [P, NCCH, 1], F32)
    lnprev = S("lnprev", [P, NCCH, 1], BF16)
    stat6 = S("stat6", [P, 2, 12], F32)
    stat2 = S("stat2", [P, 2, 2], F32)
    lnv = S("lnv", [P, 2, 1], F32)
    rstd = S("rstd", [P, 2, 1], F32)
    nmr = S("nmr", [P, 2, 1], F32)
    gw_f = S("gw_f", [P, NCCH, NE], F32)
    gw_b = S("gw_b", [P, NCCH, NE], BF16)
    g_sb = S("g_sb", [P, NE], F32)
    top8 = S("top8", [P, NE], F32)
    ind_sb = S("ind_sb", [P, NE], F32)
    cntacc = S("cntacc", [P, NE], F32)
    cnt8 = S("cnt8", [8, 1], F32)
    sT_sb = S("sT_sb", [1, NE], F32)
    s_bc = S("s_bc", [P, NE], F32)
    sbias = S("sbias", [P, NCCH, NE], F32)
    gdump = S("gdump", [P, NT, NE], F32)
    canary = S("canary", [P, 8, NCCH, NCH], F32)
    a_full = S("a_full", [P, NCCH, TC + 1], F32)
    ones_tcs = S("ones_tcs", [P, TC + 1], F32)
    xstage = S("xstage", [P, 4, C], F32)
    ystage = S("ystage", [P, 2, C], F32)
    stage = S("stage", [P, 2, C], BF16)
    att_st = S("att_st", [P, 2, C], BF16)

    wk_bf = S("wk_bf", [P, NCCH, C], BF16)
    arena0 = nc.lookup_mloc(wk_bf).addr
    wv_bf = S("wv_bf", [P, NCCH, C], BF16)
    wr_bf = S("wr_bf", [P, NCCH, C], BF16)
    wo_bf = S("wo_bf", [P, NCCH, C], BF16)
    w_bf = {"wkT": wk_bf, "wvT": wv_bf, "wrT": wr_bf, "woT": wo_bf}
    lnf = S("lnf", [P, NCCH, TC], BF16)
    dbuf = S("dbuf", [P, NCCH, TC], BF16)
    xk_b = S("xk_b", [P, 2, NCCH, TC], BF16)
    xv_b = S("xv_b", [P, 2, NCCH, TC], BF16)
    xr_b = S("xr_b", [P, 2, NCCH, TC], BF16)
    ek_b = S("ek_b", [P, NCCH, TC + 1], BF16)
    vf_b = S("vf_b", [P, NCCH, TC], BF16)
    bv_b = S("bv_b", [P, NCCH, TC + 1], BF16)
    rw_b = S("rw_b", [P, NCCH, TC], BF16)
    kf = S("kf", [P, NCCH, TC], F32)
    er = S("er", [P, NCCH, TC], F32)
    num_b = S("num_b", [P, NCCH, TC], F32)
    den_b = S("den_b", [P, NCCH, TC], F32)
    p_scan = S("p_scan", [P, NCCH, TC + 1], F32)
    q_scan = S("q_scan", [P, NCCH, TC + 1], F32)
    arena_end = nc.lookup_mloc(q_scan).addr + NCCH * (TC + 1) * 4

    AT = nc.alloc_sbuf_tensor_at
    off = arena0
    acc_t = AT("acc_t", [P, NCCH, T], F32, offset=off); off += NCCH * T * 4
    accbf_t = AT("accbf_t", [P, NCCH, T], BF16, offset=off); off += NCCH * T * 2
    ln2f = AT("ln2f", [P, NCCH, T], BF16, offset=off); off += NCCH * T * 2
    ewt_bf = AT("ewt_bf", [P, 2, NCCH, 512], BF16, offset=off); off += 2 * NCCH * 512 * 2
    h_t = AT("h_t", [P, 8, 512], F32, offset=off); off += 8 * 512 * 4
    assert off <= nc.sbuf_top, (off, arena_end, nc.sbuf_top)

    psA = nc.alloc_psum_tensor("psA", [P, 8, 512], F32)

    def zs(ap_obj, L):
        return bass.AP(ap_obj.tensor, ap_obj.offset, list(ap_obj.ap) + [[0, L]])

    def dma(engine, outap, inap, n=1, reads=(), writes=(), tag="dma", tr=False):
        if tr:
            def em(eng, sem, o=outap, i=inap):
                return eng.dma_start_transpose(o, i)
        else:
            def em(eng, sem, o=outap, i=inap):
                return eng.dma_start(out=o, in_=i)
        pl.add(engine, em, reads=reads, writes=writes, inc=16, tag=tag)

    def dma_batch(engine, pairs, reads=(), writes=(), tag="dmab", tr=False):
        def em(eng, sem, pairs=pairs, tr=tr):
            for o, i in pairs:
                if tr:
                    eng.dma_start_transpose(o, i).then_inc(sem, 16)
                else:
                    eng.dma_start(out=o, in_=i).then_inc(sem, 16)
            return None
        pl.add(engine, em, reads=reads, writes=writes, inc=16 * len(pairs), tag=tag)

    def vop(emit, reads=(), writes=(), tag="v", engine="vector"):
        pl.add(engine, emit, reads=reads, writes=writes, inc=1, tag=tag)

    dma("sync", vec_sb[:, :, :], vecs.rearrange("v p n -> p v n"), writes=["vec"])
    dma("sync", eb_sb[:, :, :], ebF, writes=["eb"])
    dma("sync", gb_sb[:, :], gbB, writes=["gb"])
    dma("sync", cc[:, :], ccol, writes=["cc"])
    dma("sync", id8_sb[:, :], id8, writes=["id8"])
    dma("sync", ones_sb[:, :], ones_row, writes=["onesr"])
    dma("gpsimd", gw_b[:, :, :], gwT.rearrange("(j p) e -> p j e", p=P),
        writes=["gwb"], tag="gwcast")
    for n in ("wkT", "wvT", "wrT", "woT"):
        dma("gpsimd", w_bf[n][:, :, :], wT[n].rearrange("(j p) d -> p j d", p=P),
            writes=["w_" + n], tag="wcast")
    for wi, fn in enumerate((AF.Exp, AF.Ln, AF.Relu, AF.Identity, AF.Exp)):
        pl.add("scalar", lambda eng, sem, fn=fn: eng.activation(
            tmp8[:, :], cc[:, 0:1].broadcast_to([P, NCCH]) if False else vec_sb[:, 0, :],
            fn, bias=cc[:, 0:1], scale=cc[:, 1:2]),
            reads=["vec", "cc"], writes=["tmp8"], tag=f"actwarm{wi}")
    pl.add("scalar", lambda eng, sem: eng.activation(
        tmp8[:, :], vec_sb[:, 0, :], AF.Exp, bias=cc[:, 1:2], scale=cc[:, 0:1]),
        reads=["vec", "cc", "tmp8"], writes=["tmp8"], tag="expw")
    pl.add("scalar", lambda eng, sem: eng.activation(
        a_sb[:, :], tmp8[:, :], AF.Exp, bias=cc[:, 1:2], scale=cc[:, 4:5]),
        reads=["tmp8", "cc"], writes=["a"], tag="expa")
    pl.add("scalar", lambda eng, sem: eng.activation(
        eu_sb[:, :], vec_sb[:, 1, :], AF.Exp, bias=cc[:, 1:2], scale=cc[:, 0:1]),
        reads=["vec", "cc"], writes=["eu"], tag="expu")
    pl.add("scalar", lambda eng, sem: eng.activation(
        bu_sb[:, :], vec_sb[:, 1, :], AF.Relu, bias=cc[:, 1:2], scale=cc[:, 0:1]),
        reads=["vec", "cc"], writes=["bu"], tag="relu_u")
    dma("sync", ones_tcs[:, :], ones_tc, writes=["onestc"], tag="z_ot")
    def em_afull(eng, sem):
        for j in range(NCCH):
            eng.activation(a_full[:, j, :], ones_tcs[:, :], AF.Identity,
                           bias=cc[:, 1:2],
                           scale=a_sb[:, j:j + 1]).then_inc(sem, 1)
        return None
    pl.add("scalar", em_afull, reads=["a", "onestc", "cc", "bu"],
           writes=["a_full"], inc=NCCH, tag="afull")
    dma("sync", m_run[:, :], zconst[:, 8:16], writes=["m_run"], tag="z_mr")
    dma("sync", p_car[:, :, 0], zconst[:, 0:8], writes=["p_car"], tag="z_pc")
    dma("sync", q_car[:, :, 0], zconst[:, 0:8], writes=["q_car"], tag="z_qc")
    dma("sync", cntacc[:, :], zconst[:, 0:8], writes=["cntacc"], tag="z_ct")
    dma("sync", lnprev[:, :, 0], zbf, writes=["lnprev"], tag="z_lp")

    def ln_stats(src_ap, reads, tag, si=0):
        vop(lambda eng, sem, s=src_ap: eng.bn_stats(stat6[:, si, 0:6], s[:, 0:512]),
            reads=reads, writes=[f"stat6a{si}"], tag=tag + "_bn0")
        vop(lambda eng, sem, s=src_ap: eng.bn_stats(stat6[:, si, 6:12], s[:, 512:1024]),
            reads=reads, writes=[f"stat6b{si}"], tag=tag + "_bn1")
        vop(lambda eng, sem: eng.bn_aggr(stat2[:, si, :], stat6[:, si, :]),
            reads=[f"stat6a{si}", f"stat6b{si}"], writes=[f"stat2{si}"],
            tag=tag + "_ag")
        pl.add("scalar", lambda eng, sem: eng.activation(
            lnv[:, si, :], stat2[:, si, 1:2], AF.Ln, bias=cc[:, 2:3],
            scale=cc[:, 0:1]),
            reads=[f"stat2{si}", "cc"], writes=[f"lnv{si}"], tag=tag + "_ln")
        pl.add("scalar", lambda eng, sem: eng.activation(
            rstd[:, si, :], lnv[:, si, :], AF.Exp, bias=cc[:, 1:2],
            scale=cc[:, 3:4]),
            reads=[f"lnv{si}", "cc"], writes=[f"rstd{si}"], tag=tag + "_ex")
        vop(lambda eng, sem: eng.scalar_tensor_tensor(
            nmr[:, si, :], stat2[:, si, 0:1], -1.0, rstd[:, si, :],
            ALU.mult, ALU.mult),
            reads=[f"stat2{si}", f"rstd{si}"], writes=[f"nmr{si}"],
            tag=tag + "_nm")

    def p1_front(c):
        t0 = c * TC
        par = c % 2
        tiles = [c * TPC + k for k in range(TPC)]
        dma_batch("sync",
                  [(xstage[:, par * 2 + k, :], xr_r[:, tiles[k], :])
                   for k in range(TPC)],
                  reads=[], writes=[f"xs{par * 2 + k}" for k in range(TPC)],
                  tag=f"xload{c}")
        for k in range(TPC):
            i = tiles[k]
            slot = par * 2 + k
            xs = xstage[:, slot, :]
            ln_stats(xs, [f"xs{slot}"], f"ln1_{i}", si=0)
            pl.add("scalar", lambda eng, sem, xs=xs, k=k: eng.activation(
                stage[:, k, :], xs, AF.Identity, bias=nmr[:, 0, :],
                scale=rstd[:, 0, :]),
                reads=["rstd0", "nmr0", f"xs{slot}"], writes=[f"stg{k}"],
                tag=f"ln1w_{i}")
            dma("sync", ln1h_r[:, i, :, :],
                stage[:, k, :].rearrange("p (j q) -> p j q", q=P),
                reads=[f"stg{k}"], writes=[f"l1h{c}"], tag=f"ln1st{i}")
        dma_batch("sync",
                  [(lnf[:, j, :], ln1h[j, t0:t0 + TC, :]) for j in range(NCCH)],
                  reads=[f"l1h{c}", "lnf"], writes=["lnf"], tag=f"l1tr{c}", tr=True)
        def em_d(eng, sem):
            eng.tensor_sub(dbuf[:, :, 1:TC], lnf[:, :, 1:TC],
                           lnf[:, :, 0:TC - 1]).then_inc(sem, 1)
            eng.tensor_sub(dbuf[:, :, 0:1], lnf[:, :, 0:1],
                           lnprev[:, :, :]).then_inc(sem, 1)
            return None
        pl.add("gpsimd", em_d, reads=["lnf", "lnprev"], writes=["dbuf"],
               inc=2, tag=f"mixd{c}")
        for dst, tokb, vi in ((xk_b, "xk", 2), (xv_b, "xv", 3), (xr_b, "xr", 4)):
            def em_mix(eng, sem, base, dst=dst, vi=vi, par=par):
                mkz = zs(vec_sb[:, vi, :], TC)
                eng.tensor_mul(dst[:, par, :, :], dbuf[:, :, :], mkz).then_inc(sem, 1)
                eng.wait_ge(sem, base + 1)
                eng.tensor_add(dst[:, par, :, 1:TC], dst[:, par, :, 1:TC],
                               lnf[:, :, 0:TC - 1]).then_inc(sem, 1)
                eng.tensor_add(dst[:, par, :, 0:1], dst[:, par, :, 0:1],
                               lnprev[:, :, :]).then_inc(sem, 1)
                return None
            em_mix._wants_base = True
            pl.add("gpsimd", em_mix, reads=["dbuf", "lnf", "lnprev", "vec",
                                            f"{tokb}{par}"],
                   writes=[f"{tokb}{par}"], inc=3, tag=f"mix{tokb}{c}")
        pl.add("gpsimd", lambda eng, sem: eng.tensor_copy(
            lnprev[:, :, :], lnf[:, :, TC - 1:TC]),
            reads=["lnf", f"xk{par}", f"xv{par}", f"xr{par}", "dbuf"],
            writes=["lnprev"], tag=f"lnprev{c}")

    def p1_back(c):
        par = c % 2
        tiles = [c * TPC + k for k in range(TPC)]
        for wname, src, tokb in (("wkT", xk_b, "xk"), ("wvT", xv_b, "xv"),
                                 ("wrT", xr_b, "xr")):
            wsb = w_bf[wname]
            for dc in range(NCCH):
                bank = dc % 4
                psv = psA[:, bank, 0:TC]
                def em_mm(eng, sem, wsb=wsb, src=src, dc=dc, psv=psv, par=par):
                    for cj in range(NCCH):
                        ins = eng.matmul(psv, wsb[:, cj, dc * P:(dc + 1) * P],
                                         src[:, par, cj, :],
                                         start=(cj == 0), stop=(cj == NCCH - 1))
                    return ins
                pl.add("tensor", em_mm,
                       reads=[f"{tokb}{par}", "w_" + wname], writes=[f"BK{bank}"],
                       tag=f"mm_{wname}{c}_{dc}")
                if wname == "wkT":
                    vop(lambda eng, sem, dc=dc, psv=psv: eng.tensor_copy(
                        kf[:, dc, :], psv), reads=[f"BK{bank}"],
                        writes=["kf"], tag=f"evk{c}_{dc}")
                elif wname == "wvT":
                    pl.add("scalar", lambda eng, sem, dc=dc, psv=psv: eng.activation(
                        vf_b[:, dc, :], psv, AF.Identity, bias=cc[:, 1:2],
                        scale=cc[:, 0:1]),
                        reads=[f"BK{bank}", "cc"], writes=["vf"],
                        tag=f"evv{c}_{dc}")
                else:
                    pl.add("scalar", lambda eng, sem, dc=dc, psv=psv: eng.activation(
                        er[:, dc, :], psv, AF.Exp, bias=cc[:, 1:2],
                        scale=cc[:, 4:5]),
                        reads=[f"BK{bank}", "cc"],
                        writes=["er"], tag=f"evr{c}_{dc}")

        if _DEBUG_CANARY:
            for st, (buf3, tok) in enumerate((
                    (xk_b, f"xk{par}"), (xv_b, f"xv{par}"), (xr_b, f"xr{par}"))):
                vop(lambda eng, sem, st=st, buf3=buf3, par=par, c=c:
                    eng.reduce_max(canary[:, st, :, c], buf3[:, par, :, :],
                                   axis=AXX, apply_absolute_value=True),
                    reads=[tok], writes=["canary"], tag=f"cn{st}_{c}")
            vop(lambda eng, sem, c=c: eng.reduce_max(
                canary[:, 3, :, c], kf[:, :, :], axis=AXX,
                apply_absolute_value=True),
                reads=["kf"], writes=["canary"], tag=f"cn3_{c}")
        vop(lambda eng, sem: eng.reduce_max(mloc[:, :], kf[:, :, :], axis=AXX),
            reads=["kf"], writes=["mloc"], tag=f"mloc{c}")
        vop(lambda eng, sem: eng.tensor_max(m_new[:, :], m_run[:, :], mloc[:, :]),
            reads=["m_run", "mloc"], writes=["m_new"], tag=f"mnew{c}")
        vop(lambda eng, sem: eng.tensor_sub(m_d[:, :], m_run[:, :], m_new[:, :]),
            reads=["m_run", "m_new"], writes=["m_d"], tag=f"md{c}")
        vop(lambda eng, sem: eng.tensor_copy(m_run[:, :], m_new[:, :]),
            reads=["m_new", "m_d"], writes=["m_run"], tag=f"mup{c}")
        pl.add("scalar", lambda eng, sem: eng.activation(
            m_r[:, :, 0], m_d[:, :], AF.Exp, bias=cc[:, 1:2], scale=cc[:, 0:1]),
            reads=["m_d", "cc"], writes=["m_r"], tag=f"mr{c}")
        vop(lambda eng, sem: eng.scalar_tensor_tensor(
            negmp[:, :], m_new[:, :], -1.0, bu_sb[:, :], ALU.mult, ALU.subtract),
            reads=["m_new", "bu"], writes=["negmp"], tag=f"negmp{c}")
        vop(lambda eng, sem: eng.tensor_mul(bv_b[:, :, 0:1], p_car[:, :, :],
                                            m_r[:, :, :]),
            reads=["p_car", "m_r", "bv"], writes=["bv"], tag=f"pini{c}")
        vop(lambda eng, sem: eng.tensor_mul(ek_b[:, :, 0:1], q_car[:, :, :],
                                            m_r[:, :, :]),
            reads=["q_car", "m_r", "ek"], writes=["ek"], tag=f"qini{c}")
        def em_ek(eng, sem):
            for dc in range(NCCH):
                eng.activation(ek_b[:, dc, 1:TC + 1], kf[:, dc, :], AF.Exp,
                               bias=negmp[:, dc:dc + 1],
                               scale=cc[:, 0:1]).then_inc(sem, 1)
            return None
        pl.add("scalar", em_ek, reads=["kf", "negmp", "cc"], writes=["ek"],
               inc=NCCH, tag=f"ek{c}")
        vop(lambda eng, sem: eng.tensor_mul(bv_b[:, :, 1:TC + 1],
                                            ek_b[:, :, 1:TC + 1],
                                            vf_b[:, :, :]),
            reads=["ek", "vf"], writes=["bv"], tag=f"bv{c}")
        if _DEBUG_CANARY:
            vop(lambda eng, sem, c=c: eng.reduce_max(
                canary[:, 4, :, c], ek_b[:, :, :], axis=AXX,
                apply_absolute_value=True),
                reads=["ek"], writes=["canary"], tag=f"cn4_{c}")
            vop(lambda eng, sem, c=c: eng.reduce_max(
                canary[:, 5, :, c], bv_b[:, :, :], axis=AXX,
                apply_absolute_value=True),
                reads=["bv"], writes=["canary"], tag=f"cn5_{c}")
            vop(lambda eng, sem, c=c: eng.reduce_max(
                canary[:, 6, :, c], er[:, :, :], axis=AXX,
                apply_absolute_value=True),
                reads=["er"], writes=["canary"], tag=f"cn6_{c}")
        def em_scanp(eng, sem):
            for dc in range(NCCH):
                eng.tensor_tensor_scan(
                    p_scan[:, dc, 0:TC + 1], a_full[:, dc, :], bv_b[:, dc, :],
                    0.0, ALU.mult, ALU.add).then_inc(sem, 1)
            return None
        pl.add("vector", em_scanp, reads=["a_full", "bv", "p_scan"],
               writes=["p_scan"], inc=NCCH, tag=f"scanp{c}")
        def em_scanq(eng, sem):
            for dc in range(NCCH):
                eng.tensor_tensor_scan(
                    q_scan[:, dc, 0:TC + 1], a_full[:, dc, :], ek_b[:, dc, :],
                    0.0, ALU.mult, ALU.add).then_inc(sem, 1)
            return None
        pl.add("vector", em_scanq, reads=["a_full", "ek", "q_scan"],
               writes=["q_scan"], inc=NCCH, tag=f"scanq{c}")
        if _DEBUG_CANARY:
            vop(lambda eng, sem, c=c: eng.reduce_max(
                canary[:, 7, :, c], p_scan[:, :, 1:TC + 1], axis=AXX,
                apply_absolute_value=True),
                reads=["p_scan"], writes=["canary"], tag=f"cn7_{c}")
        if _DEBUG_DUMPS:
            dma("sync", psd[c], p_scan[:, :, :], reads=["p_scan"],
                writes=[], tag=f"dpsd{c}")
            dma("sync", ekd[c], ek_b[:, :, :], reads=["ek"],
                writes=[], tag=f"dekd{c}")
            dma("sync", vfd[c], vf_b[:, :, :], reads=["vf"],
                writes=[], tag=f"dvfd{c}")
        vop(lambda eng, sem: eng.tensor_copy(p_car[:, :, :], p_scan[:, :, TC:TC + 1]),
            reads=["p_scan"], writes=["p_car"], tag=f"pcar{c}")
        vop(lambda eng, sem: eng.tensor_copy(q_car[:, :, :], q_scan[:, :, TC:TC + 1]),
            reads=["q_scan"], writes=["q_car"], tag=f"qcar{c}")
        def em_num(eng, sem, base):
            euz = zs(eu_sb[:, :], TC)
            eng.tensor_mul(num_b[:, :, :], bv_b[:, :, 1:TC + 1], euz).then_inc(sem, 1)
            eng.wait_ge(sem, base + 1)
            eng.tensor_add(num_b[:, :, :], num_b[:, :, :],
                           p_scan[:, :, 0:TC]).then_inc(sem, 1)
            return None
        em_num._wants_base = True
        pl.add("gpsimd", em_num, reads=["bv", "eu", "p_scan"], writes=["num"],
               inc=2, tag=f"num{c}")
        def em_den(eng, sem, base):
            euz = zs(eu_sb[:, :], TC)
            eng.tensor_mul(den_b[:, :, :], ek_b[:, :, 1:TC + 1], euz).then_inc(sem, 1)
            eng.wait_ge(sem, base + 1)
            eng.tensor_add(den_b[:, :, :], den_b[:, :, :],
                           q_scan[:, :, 0:TC]).then_inc(sem, 1)
            return None
        em_den._wants_base = True
        pl.add("gpsimd", em_den, reads=["ek", "eu", "q_scan"], writes=["den"],
               inc=2, tag=f"den{c}")
        pl.add("scalar", lambda eng, sem: eng.activation(
            kf[:, :, :], er[:, :, :], AF.Ln, bias=cc[:, 0:1], scale=cc[:, 0:1]),
            reads=["er", "cc", "ek"], writes=["kf"], tag=f"srl{c}")
        pl.add("scalar", lambda eng, sem: eng.activation(
            er[:, :, :], kf[:, :, :], AF.Exp, bias=cc[:, 1:2], scale=cc[:, 4:5]),
            reads=["kf", "cc"], writes=["er"], tag=f"sre{c}")
        pl.add("scalar", lambda eng, sem: eng.activation(
            kf[:, :, :], den_b[:, :, :], AF.Ln, bias=cc[:, 1:2], scale=cc[:, 0:1]),
            reads=["den", "cc", "er"], writes=["kf"], tag=f"rdl{c}")
        pl.add("scalar", lambda eng, sem: eng.activation(
            den_b[:, :, :], kf[:, :, :], AF.Exp, bias=cc[:, 1:2], scale=cc[:, 4:5]),
            reads=["kf", "cc"], writes=["den"], tag=f"rde{c}")
        vop(lambda eng, sem: eng.tensor_mul(num_b[:, :, :], num_b[:, :, :],
                                            den_b[:, :, :]),
            reads=["num", "den"], writes=["num"], tag=f"wkv{c}")
        vop(lambda eng, sem: eng.tensor_mul(rw_b[:, :, :], num_b[:, :, :],
                                            er[:, :, :]),
            reads=["num", "er"], writes=["rw"], tag=f"rwkv{c}")
        if _DEBUG_DUMPS:
            dma("sync", rwd[c], rw_b[:, :, :], reads=["rw"],
                writes=[], tag=f"drwd{c}")

        for k in range(TPC):
            i = tiles[k]
            slot = par * 2 + k
            def em_wo(eng, sem, k=k):
                for cj in range(NCCH):
                    for hh in range(2):
                        ins = eng.matmul(
                            psA[:, 4 + hh, :],
                            rw_b[:, cj, k * P:(k + 1) * P],
                            wo_bf[:, cj, hh * 512:(hh + 1) * 512],
                            start=(cj == 0), stop=(cj == NCCH - 1))
                return ins
            pl.add("tensor", em_wo, reads=["rw", "w_woT"],
                   writes=["BK4", "BK5"], tag=f"wo{c}_{k}")
            vop(lambda eng, sem: eng.bn_stats(stat6[:, 1, 0:6], psA[:, 4, :]),
                reads=["BK4"], writes=["stat6a1"], tag=f"wob0{i}")
            vop(lambda eng, sem: eng.bn_stats(stat6[:, 1, 6:12], psA[:, 5, :]),
                reads=["BK5"], writes=["stat6b1"], tag=f"wob1{i}")
            vop(lambda eng, sem: eng.bn_aggr(stat2[:, 1, :], stat6[:, 1, :]),
                reads=["stat6a1", "stat6b1"], writes=["stat21"], tag=f"woag{i}")
            pl.add("scalar", lambda eng, sem: eng.activation(
                lnv[:, 1, :], stat2[:, 1, 1:2], AF.Ln, bias=cc[:, 2:3],
                scale=cc[:, 0:1]),
                reads=["stat21", "cc"], writes=["lnv1"], tag=f"woln{i}")
            pl.add("scalar", lambda eng, sem: eng.activation(
                rstd[:, 1, :], lnv[:, 1, :], AF.Exp, bias=cc[:, 1:2],
                scale=cc[:, 3:4]),
                reads=["lnv1", "cc"], writes=["rstd1"], tag=f"woex{i}")
            vop(lambda eng, sem: eng.scalar_tensor_tensor(
                nmr[:, 1, :], stat2[:, 1, 0:1], -1.0, rstd[:, 1, :],
                ALU.mult, ALU.mult),
                reads=["stat21", "rstd1"], writes=["nmr1"], tag=f"wonm{i}")
            pl.add("scalar", lambda eng, sem, k=k: eng.activation(
                ystage[:, k, :], psA[:, 4:6, :], AF.Identity, bias=nmr[:, 1, :],
                scale=rstd[:, 1, :]),
                reads=["rstd1", "nmr1", "BK4", "BK5"], writes=[f"y{k}"],
                tag=f"woy{i}")
            vop(lambda eng, sem, k=k, slot=slot: eng.tensor_add(
                ystage[:, k, :], ystage[:, k, :], xstage[:, slot, :]),
                reads=[f"y{k}", f"xs{slot}"], writes=[f"y{k}"], tag=f"wox{i}")
            dma("sync", out_r[:, i, :], ystage[:, k, :],
                reads=[f"y{k}"], writes=[f"oh{i}"], tag=f"x1st{i}")

    p1_front(0)
    for c in range(NCH):
        if c + 1 < NCH:
            p1_front(c + 1)
        p1_back(c)

    pl.barrier("ph_ffn")

    for c in range(NCH):
        t0 = c * TC
        tiles = [c * TPC + k for k in range(TPC)]
        dma_batch("sync",
                  [(xstage[:, (c % 2) * 2 + k, :], out_r[:, tiles[k], :])
                   for k in range(TPC)],
                  reads=[f"oh{t}" for t in tiles] + [f"xs{(c % 2) * 2 + k}" for k in range(TPC)],
                  writes=[f"xs{(c % 2) * 2 + k}" for k in range(TPC)],
                  tag=f"x2load{c}")
        for k in range(TPC):
            i = tiles[k]
            slot = (c % 2) * 2 + k
            xs = xstage[:, slot, :]
            ln_stats(xs, [f"xs{slot}"], f"ln2_{i}", si=0)
            pl.add("scalar", lambda eng, sem, xs=xs, k=k: eng.activation(
                stage[:, k, :], xs, AF.Identity, bias=nmr[:, 0, :],
                scale=rstd[:, 0, :]),
                reads=["rstd0", "nmr0", f"xs{slot}"], writes=[f"stg{k}"],
                tag=f"ln2w_{i}")
            dma("sync", ln2h_r[:, i, :, :],
                stage[:, k, :].rearrange("p (j q) -> p j q", q=P),
                reads=[f"stg{k}"], writes=[f"l2h{c}"], tag=f"ln2st{i}")
        dma_batch("sync",
                  [(ln2f[:, j, t0:t0 + TC], ln2h[j, t0:t0 + TC, :])
                   for j in range(NCCH)],
                  reads=[f"l2h{c}"], writes=[f"l2f{c}"], tag=f"l2tr{c}", tr=True)
        for k in range(TPC):
            i = tiles[k]
            gbk = 6 + (i % 2)
            def em_gmm(eng, sem, i=i, gbk=gbk):
                for cj in range(NCCH):
                    ins = eng.matmul(psA[:, gbk, 0:NE],
                                     ln2f[:, cj, i * P:(i + 1) * P],
                                     gw_b[:, cj, :],
                                     start=(cj == 0), stop=(cj == NCCH - 1))
                return ins
            pl.add("tensor", em_gmm, reads=[f"l2f{c}", "gwb"],
                   writes=[f"BK{gbk}"], tag=f"gmm{i}")
            vop(lambda eng, sem, gbk=gbk: eng.tensor_add(g_sb[:, :],
                                                psA[:, gbk, 0:NE],
                                                gb_sb[:, :]),
                reads=[f"BK{gbk}", "gb"], writes=["g"], tag=f"gadd{i}")
            vop(lambda eng, sem, i=i: eng.tensor_copy(gdump[:, i, :], g_sb[:, :]),
                reads=["g"], writes=["gdump"], tag=f"gdmp{i}")
            vop(lambda eng, sem: eng.max(top8[:, :], g_sb[:, :]),
                reads=["g"], writes=["top8"], tag=f"gtop{i}")
            vop(lambda eng, sem: eng.tensor_scalar(ind_sb[:, :], g_sb[:, :],
                                                   top8[:, 1:2], None, ALU.is_ge),
                reads=["g", "top8"], writes=["ind"], tag=f"gind{i}")
            vop(lambda eng, sem: eng.tensor_add(cntacc[:, :], cntacc[:, :],
                                                ind_sb[:, :]),
                reads=["cntacc", "ind"], writes=["cntacc"], tag=f"gcnt{i}")
    pl.add("tensor", lambda eng, sem: eng.matmul(
        psA[0:8, 5, 0:1], cntacc[:, :], cc[:, 5:6], start=True, stop=True),
        reads=["cntacc", "cc"], writes=["BK5"], tag="cntmm")
    vop(lambda eng, sem: eng.tensor_copy(cnt8[:, :], psA[0:8, 5, 0:1]),
        reads=["BK5"], writes=["cnt8"], tag="cntev")
    dma("gpsimd", cc_in.ap(), cnt8[:, :], reads=["cnt8"], writes=["ccio"],
        tag="ccout")
    pl.add("gpsimd", lambda eng, sem: eng.collective_compute(
        "AllReduce", ALU.add, replica_groups=[list(range(NCORES))],
        ins=[cc_in.ap()], outs=[cc_out.ap()]),
        reads=["ccio"], writes=["ccres"], tag="allred")
    dma("gpsimd", cnt8[:, :], cc_out.ap(), reads=["ccres", "cnt8"],
        writes=["cnt8"], tag="ccin")
    pl.add("tensor", lambda eng, sem: eng.matmul(
        psA[0:1, 5, 8:16], cnt8[:, :], id8_sb[:, :], start=True, stop=True),
        reads=["cnt8", "id8"], writes=["BK5"], tag="cnttr")
    vop(lambda eng, sem: eng.tensor_scalar(sT_sb[:, :], psA[0:1, 5, 8:16],
                                           INV_TOTAL, None, ALU.mult),
        reads=["BK5"], writes=["sT"], tag="sT")
    pl.add("tensor", lambda eng, sem: eng.matmul(
        psA[:, 5, 16:24], ones_sb[:, :], sT_sb[:, :], start=True, stop=True),
        reads=["sT", "onesr"], writes=["BK5"], tag="sbc_mm")
    vop(lambda eng, sem: eng.tensor_copy(s_bc[:, :], psA[:, 5, 16:24]),
        reads=["BK5"], writes=["s_bc"], tag="sbc_ev")
    def em_sbias(eng, sem):
        for dc in range(NCCH):
            eng.tensor_mul(sbias[:, dc, :], eb_sb[:, dc, :],
                           s_bc[:, :]).then_inc(sem, 1)
        return None
    pl.add("vector", em_sbias, reads=["eb", "s_bc"], writes=["sbias"],
           inc=NCCH, tag="sbias")

    l2deps = [f"l2f{c}" for c in range(NCH)]
    for e_i in range(NE):
        for half in range(2):
            par = (2 * e_i + half) % 2
            dma("gpsimd", ewt_bf[:, par, :, :],
                ewT[e_i][:, half * 512:(half + 1) * 512]
                .rearrange("(j p) d -> p j d", p=P),
                reads=[f"ew{par}"], writes=[f"ew{par}"], tag=f"ewl{e_i}_{half}")
            for dq in range(4):
                dc = half * 4 + dq
                b0 = 4 * (dc % 2)
                def em_emm(eng, sem, par=par, dq=dq, b0=b0):
                    for cj in range(NCCH):
                        for tq in range(4):
                            ins = eng.matmul(
                                psA[:, b0 + tq, :],
                                ewt_bf[:, par, cj, dq * P:(dq + 1) * P],
                                ln2f[:, cj, tq * 512:(tq + 1) * 512],
                                start=(cj == 0), stop=(cj == NCCH - 1))
                    return ins
                bks = [f"BK{b0 + q}" for q in range(4)]
                pl.add("tensor", em_emm,
                       reads=l2deps + [f"ew{par}"],
                       writes=bks, tag=f"emm{e_i}_{dc}")
                for tq in range(4):
                    sl = (dc % 2) * 4 + tq
                    col = slice(tq * 512, (tq + 1) * 512)
                    if e_i == 0:
                        pl.add("scalar", lambda eng, sem, dc=dc, b0=b0, tq=tq, col=col:
                               eng.activation(acc_t[:, dc, col], psA[:, b0 + tq, :],
                                              AF.Relu, bias=sbias[:, dc, 0:1],
                                              scale=s_bc[:, 0:1]),
                               reads=[f"BK{b0 + tq}", "sbias", "s_bc"],
                               writes=[f"acc{dc}"], tag=f"eev{e_i}_{dc}_{tq}")
                    else:
                        pl.add("scalar", lambda eng, sem, dc=dc, b0=b0, tq=tq, e_i=e_i, sl=sl:
                               eng.activation(h_t[:, sl, :], psA[:, b0 + tq, :],
                                              AF.Relu, bias=sbias[:, dc, e_i:e_i + 1],
                                              scale=s_bc[:, e_i:e_i + 1]),
                               reads=[f"BK{b0 + tq}", "sbias", "s_bc", f"h{sl}"],
                               writes=[f"h{sl}"], tag=f"eev{e_i}_{dc}_{tq}")
                        addeng = "vector" if (dc % 2 == 0) else "gpsimd"
                        if e_i < NE - 1:
                            pl.add(addeng, lambda eng, sem, dc=dc, col=col, sl=sl:
                                   eng.tensor_add(acc_t[:, dc, col],
                                                  acc_t[:, dc, col], h_t[:, sl, :]),
                                   reads=[f"acc{dc}", f"h{sl}"],
                                   writes=[f"acc{dc}", f"h{sl}"],
                                   tag=f"eadd{e_i}_{dc}_{tq}")
                        else:
                            pl.add(addeng, lambda eng, sem, dc=dc, col=col, sl=sl:
                                   eng.tensor_add(accbf_t[:, dc, col],
                                                  acc_t[:, dc, col], h_t[:, sl, :]),
                                   reads=[f"acc{dc}", f"h{sl}"],
                                   writes=["accbf", f"h{sl}"],
                                   tag=f"eaddf{e_i}_{dc}_{tq}")

    dma_batch("sync",
              [(acch_r[:, j, :, :],
                accbf_t[:, j, :].rearrange("p (i q) -> p i q", q=P))
               for j in range(NCCH)],
              reads=["accbf"], writes=["acch"], tag="accst")

    for i in range(NT):
        sl = i % 2
        dma("sync", att_st[:, sl, :], acch[i, :, :],
            reads=["acch", f"at{sl}"], writes=[f"at{sl}"], tag=f"attr{i}", tr=True)
        dma("sync", xstage[:, sl, :], out_r[:, i, :],
            reads=[f"oh{i}", f"xs{sl}"], writes=[f"xs{sl}"], tag=f"x4l{i}")
        ln_stats(att_st[:, sl, :], [f"at{sl}"], f"ln4_{i}", si=0)
        pl.add("scalar", lambda eng, sem, sl=sl: eng.activation(
            ystage[:, sl, :], att_st[:, sl, :], AF.Identity, bias=nmr[:, 0, :],
            scale=rstd[:, 0, :]),
            reads=["rstd0", "nmr0", f"at{sl}", f"y4{sl}"], writes=[f"y4{sl}"],
            tag=f"y2w{i}")
        vop(lambda eng, sem, sl=sl: eng.tensor_add(
            ystage[:, sl, :], ystage[:, sl, :], xstage[:, sl, :]),
            reads=[f"y4{sl}", f"xs{sl}"], writes=[f"y4{sl}"], tag=f"y2x{i}")
        dma("sync", out_r[:, i, :], ystage[:, sl, :],
            reads=[f"y4{sl}"], writes=[f"oh{i}", "outfin"], tag=f"ost{i}")

    tensors = dict(kf=kf, er=er, num_b=num_b, den_b=den_b, p_scan=p_scan,
                   q_scan=q_scan, ek_b=ek_b, bv_b=bv_b, rw_b=rw_b, lnf=lnf,
                   xk_b=xk_b, xv_b=xv_b, xr_b=xr_b, ln2f=ln2f, acc_t=acc_t,
                   s_bc=s_bc, cnt8=cnt8, g_sb=g_sb, cntacc=cntacc, ind_sb=ind_sb, top8=top8, gdump=gdump, gw_f=gw_f, gw_b=gw_b, canary=canary,
                   accbf_t=accbf_t, vf_b=vf_b, a_full=a_full, a_sb=a_sb,
                   m_run=m_run, eu_sb=eu_sb)
    dram_taps = {"ln1h": ln1h, "ln2h": ln2h, "acch": acch}
    for name, shape, dtype, key in _DEBUG_TAPS:
        tdr = nc.dram_tensor("tap_" + name, shape, dtype, kind="ExternalOutput").ap()
        if key.startswith("dram:"):
            src_ap = dram_taps[key[5:]].ap()
        else:
            src = tensors[key]
            src_ap = src[(slice(None),) * len(src.shape)]
        dma("sync", tdr, src_ap, reads=["outfin"], writes=[], tag="tap_" + name)

    pl.add("sync", lambda eng, sem: eng.nop(), reads=["outfin"], writes=[],
           tag="final")

    pl.schedule()
    import contextlib
    with contextlib.ExitStack() as ctx:
        sems = {en: ctx.enter_context(nc.semaphore("sem_" + en))
                for en in ENGINES}
        with nc.Block() as block:
            block.sync(lambda eng: pl.emit_engine("sync", eng, sems))
            block.tensor(lambda eng: pl.emit_engine("tensor", eng, sems))
            block.vector(lambda eng: pl.emit_engine("vector", eng, sems))
            block.scalar(lambda eng: pl.emit_engine("scalar", eng, sems))
            block.gpsimd(lambda eng: pl.emit_engine("gpsimd", eng, sems))
    return nc


def _host_layout(inputs):
    x = np.asarray(inputs["L0_attn_x"])
    base = {
        "wkT": np.ascontiguousarray(np.asarray(inputs["L0_attn_Wk"]).T),
        "wvT": np.ascontiguousarray(np.asarray(inputs["L0_attn_Wv"]).T),
        "wrT": np.ascontiguousarray(np.asarray(inputs["L0_attn_Wr"]).T),
        "woT": np.ascontiguousarray(np.asarray(inputs["L0_attn_Wo"]).T),
        "ewT": np.ascontiguousarray(
            np.asarray(inputs["L0_ffn_expert_w"]).transpose(0, 2, 1)),
        "gwT": np.ascontiguousarray(np.asarray(inputs["L0_ffn_gate_w"]).T),
        "vecs": np.ascontiguousarray(np.stack([
            np.asarray(inputs["L0_attn_w"]).reshape(NCCH, P).T,
            np.asarray(inputs["L0_attn_u"]).reshape(NCCH, P).T,
            np.asarray(inputs["L0_attn_mix_k"]).reshape(NCCH, P).T,
            np.asarray(inputs["L0_attn_mix_v"]).reshape(NCCH, P).T,
            np.asarray(inputs["L0_attn_mix_r"]).reshape(NCCH, P).T,
        ])),
        "ebF": np.ascontiguousarray(
            np.asarray(inputs["L0_ffn_expert_b"]).reshape(NE, NCCH, P)
            .transpose(2, 1, 0)),
        "gbB": np.ascontiguousarray(np.broadcast_to(
            np.asarray(inputs["L0_ffn_gate_b"])[None, :], (P, NE))),
        "ccol": np.ascontiguousarray(np.broadcast_to(np.array(
            [1.0, 0.0, EPS, -0.5, -1.0, 1.0, 0.0, 0.0], np.float32)[None, :],
            (P, 8))),
        "id8": np.eye(8, dtype=np.float32),
        "ones_row": np.ones((1, P), np.float32),
        "zconst": np.concatenate([np.zeros((P, 8), np.float32),
                                  np.full((P, 8), -1e30, np.float32)], axis=1),
        "ones_tc": np.ones((P, TC + 1), np.float32),
    }
    base = {k: np.ascontiguousarray(v, dtype=np.float32) for k, v in base.items()}
    base["zbf"] = np.zeros((P, 8), ml_dtypes.bfloat16)
    maps = []
    for core in range(NCORES):
        m = dict(base)
        m["x"] = np.ascontiguousarray(x[core], dtype=np.float32)
        maps.append(m)
    return maps


def kernel(**inputs):
    maps = _host_layout(inputs)
    nc = build_nc()
    res = run_bass_kernel_spmd(nc, maps, core_ids=list(range(NCORES)))
    kernel.raw_results = res.results
    outp = np.stack([np.asarray(res.results[c]["out"]) for c in range(NCORES)])
    if _DEBUG_TAPS:
        kernel.tap_results = [
            {("tap_" + n): np.asarray(res.results[c]["tap_" + n])
             for n, _, _, _ in _DEBUG_TAPS} for c in range(NCORES)]
    return outp.astype(np.float32)


# version 12
# speedup vs baseline: 1.0733x; 1.0733x over previous
import numpy as np
import ml_dtypes
import concourse.bass as bass
import concourse.mybir as mybir
from concourse.bass_utils import run_bass_kernel_spmd

F32 = mybir.dt.float32
BF16 = mybir.dt.bfloat16
AF = mybir.ActivationFunctionType
ALU = mybir.AluOpType
AXX = mybir.AxisListType.X

P = 128
T = 2048
C = 1024
NCCH = 8
NT = 16
NCORES = 8
TC = 256
NCH = T // TC
TPC = TC // P
NE = 8
EPS = 1e-5
INV_TOTAL = 1.0 / 32768.0

ENGINES = ("sync", "tensor", "vector", "scalar", "gpsimd")
NSYNCP = 12
NGPSP = 6
SEMS = list(ENGINES) + [f"sd{i}" for i in range(NSYNCP)] + \
       [f"gd{i}" for i in range(NGPSP)] + ["ccs"]

_DEBUG_TAPS = []
_DEBUG_CANARY = False
_DEBUG_DUMPS = False


class _Op:
    __slots__ = ("engine", "emit", "reads", "writes", "inc", "waits", "tag",
                 "base", "sem")

    def __init__(self, engine, emit, reads, writes, inc, tag, sem):
        self.engine = engine
        self.emit = emit
        self.reads = reads
        self.writes = writes
        self.inc = inc
        self.waits = None
        self.tag = tag
        self.base = 0
        self.sem = sem


class Plan:
    def __init__(self):
        self.ops = []
        self.barrier_tok = None

    def add(self, engine, emit, reads=(), writes=(), inc=1, tag="", sem=None):
        r = list(reads)
        if self.barrier_tok is not None:
            r.append(self.barrier_tok)
        self.ops.append(_Op(engine, emit, r, list(writes), inc, tag,
                            sem or engine))

    def barrier(self, tok):
        bt = self.barrier_tok
        for en in ENGINES:
            self.add(en, lambda eng, sem: eng.nop(), reads=([bt] if bt else []),
                     writes=[tok], tag="barrier")
        self.barrier_tok = tok

    def schedule(self):
        last_writer = {}
        readers = {}
        cum = {s: 0 for s in SEMS}
        for op in self.ops:
            deps = {}
            for r in op.reads:
                w = last_writer.get(r)
                if w is not None:
                    deps[w[0]] = max(deps.get(w[0], 0), w[1])
            for w_ in op.writes:
                w = last_writer.get(w_)
                if w is not None:
                    deps[w[0]] = max(deps.get(w[0], 0), w[1])
                for sm, cval in readers.get(w_, ()):
                    deps[sm] = max(deps.get(sm, 0), cval)
            op.base = cum[op.sem]
            if op.sem != op.engine and op.base > 0:
                deps[op.sem] = max(deps.get(op.sem, 0), op.base)
            cum[op.sem] += op.inc
            my = (op.sem, cum[op.sem])
            for r in op.reads:
                readers.setdefault(r, []).append(my)
            for w_ in op.writes:
                last_writer[w_] = my
                readers[w_] = []
            op.waits = deps
        self.final = dict(cum)

    def emit_engine(self, engine_name, eng, sems):
        watermark = {s: 0 for s in SEMS}
        for op in self.ops:
            if op.engine != engine_name:
                continue
            for sm in sorted(op.waits):
                c = op.waits[sm]
                if c > watermark[sm]:
                    eng.wait_ge(sems[sm], c)
                    watermark[sm] = c
            if getattr(op.emit, "_wants_base", False):
                ins = op.emit(eng, sems[op.sem], op.base)
            else:
                ins = op.emit(eng, sems[op.sem])
            if ins is not None:
                ins.then_inc(sems[op.sem], op.inc)


def build_nc():
    nc = bass.Bass("TRN2", target_bir_lowering=False)
    pl = Plan()

    xin = nc.dram_tensor("x", [T, C], F32, kind="ExternalInput").ap()
    wT = {n: nc.dram_tensor(n, [C, C], F32, kind="ExternalInput").ap()
          for n in ("wkT", "wvT", "wrT", "woT")}
    ewT = nc.dram_tensor("ewT", [NE, C, C], F32, kind="ExternalInput").ap()
    gwT = nc.dram_tensor("gwT", [C, NE], F32, kind="ExternalInput").ap()
    vecs = nc.dram_tensor("vecs", [5, P, NCCH], F32, kind="ExternalInput").ap()
    ebF = nc.dram_tensor("ebF", [P, NCCH, NE], F32, kind="ExternalInput").ap()
    gbB = nc.dram_tensor("gbB", [P, NE], F32, kind="ExternalInput").ap()
    ccol = nc.dram_tensor("ccol", [P, 8], F32, kind="ExternalInput").ap()
    id8 = nc.dram_tensor("id8", [8, 8], F32, kind="ExternalInput").ap()
    ones_row = nc.dram_tensor("ones_row", [1, P], F32, kind="ExternalInput").ap()
    zconst = nc.dram_tensor("zconst", [P, 16], F32, kind="ExternalInput").ap()
    ones_tc = nc.dram_tensor("ones_tc", [P, TC + 1], F32, kind="ExternalInput").ap()
    zbf = nc.dram_tensor("zbf", [P, 8], BF16, kind="ExternalInput").ap()
    out = nc.dram_tensor("out", [T, C], F32, kind="ExternalOutput").ap()

    if _DEBUG_DUMPS:
        ekd = nc.dram_tensor("tap_ekd", [NCH, P, NCCH, TC + 1], BF16,
                             kind="ExternalOutput")
        rwd = nc.dram_tensor("tap_rwd", [NCH, P, NCCH, TC], BF16,
                             kind="ExternalOutput")
        psd = nc.dram_tensor("tap_psd", [NCH, P, NCCH, TC + 1], F32,
                             kind="ExternalOutput")
        vfd = nc.dram_tensor("tap_vfd", [NCH, P, NCCH, TC], BF16,
                             kind="ExternalOutput")
    ewh = nc.dram_tensor("ewh", [NE, C, C], BF16)
    ln1h = nc.dram_tensor("ln1h", [NCCH, T, P], BF16)
    ln2h = nc.dram_tensor("ln2h", [NCCH, T, P], BF16)
    acch = nc.dram_tensor("acch", [NT, C, P], BF16)
    cc_in = nc.dram_tensor("cc_in", [8, 1], F32)
    cc_out = nc.dram_tensor("cc_out", [8, 1], F32, addr_space="Shared")

    xr_r = xin.rearrange("(n p) c -> p n c", p=P)
    out_r = out.rearrange("(n p) c -> p n c", p=P)
    ln1h_r = ln1h.rearrange("j (n p) q -> p n j q", p=P)
    ln2h_r = ln2h.rearrange("j (n p) q -> p n j q", p=P)
    acch_r = acch.rearrange("i (j p) q -> p j i q", p=P)

    S = nc.alloc_sbuf_tensor
    vec_sb = S("vec_sb", [P, 5, NCCH], F32)
    eb_sb = S("eb_sb", [P, NCCH, NE], F32)
    gb_sb = S("gb_sb", [P, NE], F32)
    cc = S("cc", [P, 8], F32)
    id8_sb = S("id8_sb", [8, 8], F32)
    ones_sb = S("ones_sb", [1, P], F32)
    a_sb = S("a_sb", [P, NCCH], F32)
    eu_sb = S("eu_sb", [P, NCCH], F32)
    bu_sb = S("bu_sb", [P, NCCH], F32)
    tmp8 = S("tmp8", [P, NCCH], F32)
    m_run = S("m_run", [P, NCCH], F32)
    m_new = S("m_new", [P, NCCH], F32)
    m_d = S("m_d", [P, NCCH], F32)
    m_r = S("m_r", [P, NCCH, 1], F32)
    negmp = S("negmp", [P, NCCH], F32)
    mloc = S("mloc", [P, NCCH], F32)
    p_car = S("p_car", [P, NCCH, 1], F32)
    q_car = S("q_car", [P, NCCH, 1], F32)
    lnprev = S("lnprev", [P, NCCH, 1], BF16)
    stat6 = S("stat6", [P, 2, 12], F32)
    stat2 = S("stat2", [P, 2, 2], F32)
    lnv = S("lnv", [P, 2, 1], F32)
    rstd = S("rstd", [P, 2, 1], F32)
    nmr = S("nmr", [P, 2, 1], F32)
    gw_f = S("gw_f", [P, NCCH, NE], F32)
    gw_b = S("gw_b", [P, NCCH, NE], BF16)
    g_sb = S("g_sb", [P, NE], F32)
    top8 = S("top8", [P, NE], F32)
    ind_sb = S("ind_sb", [P, NE], F32)
    cntacc = S("cntacc", [P, NE], F32)
    cnt8 = S("cnt8", [8, 1], F32)
    sT_sb = S("sT_sb", [1, NE], F32)
    s_bc = S("s_bc", [P, NE], F32)
    sbias = S("sbias", [P, NCCH, NE], F32)
    gdump = S("gdump", [P, NT, NE], F32)
    canary = S("canary", [P, 8, NCCH, NCH], F32)
    a_full = S("a_full", [P, NCCH, TC + 1], F32)
    ones_tcs = S("ones_tcs", [P, TC + 1], F32)
    xstage = S("xstage", [P, 4, C], F32)
    ystage = S("ystage", [P, 2, C], F32)
    stage = S("stage", [P, 2, C], BF16)
    att_st = S("att_st", [P, 2, C], BF16)

    wk_bf = S("wk_bf", [P, NCCH, C], BF16)
    arena0 = nc.lookup_mloc(wk_bf).addr
    wv_bf = S("wv_bf", [P, NCCH, C], BF16)
    wr_bf = S("wr_bf", [P, NCCH, C], BF16)
    wo_bf = S("wo_bf", [P, NCCH, C], BF16)
    w_bf = {"wkT": wk_bf, "wvT": wv_bf, "wrT": wr_bf, "woT": wo_bf}
    lnf = S("lnf", [P, NCCH, TC], BF16)
    dbuf = S("dbuf", [P, NCCH, TC], BF16)
    xk_b = S("xk_b", [P, 2, NCCH, TC], BF16)
    xv_b = S("xv_b", [P, 2, NCCH, TC], BF16)
    xr_b = S("xr_b", [P, 2, NCCH, TC], BF16)
    ek_b = S("ek_b", [P, NCCH, TC + 1], BF16)
    vf_b = S("vf_b", [P, NCCH, TC], BF16)
    bv_b = S("bv_b", [P, NCCH, TC + 1], BF16)
    rw_b = S("rw_b", [P, NCCH, TC], BF16)
    kf = S("kf", [P, NCCH, TC], F32)
    er = S("er", [P, NCCH, TC], F32)
    num_b = S("num_b", [P, NCCH, TC], F32)
    den_b = S("den_b", [P, NCCH, TC], F32)
    p_scan = S("p_scan", [P, NCCH, TC + 1], F32)
    q_scan = S("q_scan", [P, NCCH, TC + 1], F32)
    arena_end = nc.lookup_mloc(q_scan).addr + NCCH * (TC + 1) * 4

    AT = nc.alloc_sbuf_tensor_at
    off = arena0
    acc_t = AT("acc_t", [P, NCCH, T], F32, offset=off); off += NCCH * T * 4
    accbf_t = AT("accbf_t", [P, NCCH, T], BF16, offset=off); off += NCCH * T * 2
    ln2f = AT("ln2f", [P, NCCH, T], BF16, offset=off); off += NCCH * T * 2
    ewt_bf = AT("ewt_bf", [P, 2, NCCH, 512], BF16, offset=off); off += 2 * NCCH * 512 * 2
    h_t = AT("h_t", [P, 8, 512], F32, offset=off); off += 8 * 512 * 4
    assert off <= nc.sbuf_top, (off, arena_end, nc.sbuf_top)

    psA = nc.alloc_psum_tensor("psA", [P, 8, 512], F32)

    def zs(ap_obj, L):
        return bass.AP(ap_obj.tensor, ap_obj.offset, list(ap_obj.ap) + [[0, L]])

    dma_rr = {"sync": [0], "gpsimd": [0]}

    def _next_sem(engine):
        if engine == "sync":
            i = dma_rr["sync"][0] % NSYNCP
            dma_rr["sync"][0] += 1
            return f"sd{i}"
        else:
            i = dma_rr["gpsimd"][0] % NGPSP
            dma_rr["gpsimd"][0] += 1
            return f"gd{i}"

    def dma(engine, outap, inap, n=1, reads=(), writes=(), tag="dma", tr=False):
        if tr:
            def em(eng, sem, o=outap, i=inap):
                return eng.dma_start_transpose(o, i)
        else:
            def em(eng, sem, o=outap, i=inap):
                return eng.dma_start(out=o, in_=i)
        pl.add(engine, em, reads=reads, writes=writes, inc=16, tag=tag,
               sem=_next_sem(engine))

    def dma_batch(engine, pairs, reads=(), writes=(), tag="dmab", tr=False):
        def em(eng, sem, pairs=pairs, tr=tr):
            for o, i in pairs:
                if tr:
                    eng.dma_start_transpose(o, i).then_inc(sem, 16)
                else:
                    eng.dma_start(out=o, in_=i).then_inc(sem, 16)
            return None
        pl.add(engine, em, reads=reads, writes=writes, inc=16 * len(pairs),
               tag=tag, sem=_next_sem(engine))

    def vop(emit, reads=(), writes=(), tag="v", engine="vector"):
        pl.add(engine, emit, reads=reads, writes=writes, inc=1, tag=tag)

    dma("sync", vec_sb[:, :, :], vecs.rearrange("v p n -> p v n"), writes=["vec"])
    dma("sync", eb_sb[:, :, :], ebF, writes=["eb"])
    dma("sync", gb_sb[:, :], gbB, writes=["gb"])
    dma("sync", cc[:, :], ccol, writes=["cc"])
    dma("sync", id8_sb[:, :], id8, writes=["id8"])
    dma("sync", ones_sb[:, :], ones_row, writes=["onesr"])
    dma("gpsimd", gw_b[:, :, :], gwT.rearrange("(j p) e -> p j e", p=P),
        writes=["gwb"], tag="gwcast")
    for n in ("wkT", "wvT", "wrT", "woT"):
        dma("gpsimd", w_bf[n][:, :, :], wT[n].rearrange("(j p) d -> p j d", p=P),
            writes=["w_" + n], tag="wcast")
    for wi, fn in enumerate((AF.Exp, AF.Ln, AF.Relu, AF.Identity, AF.Exp)):
        pl.add("scalar", lambda eng, sem, fn=fn: eng.activation(
            tmp8[:, :], cc[:, 0:1].broadcast_to([P, NCCH]) if False else vec_sb[:, 0, :],
            fn, bias=cc[:, 0:1], scale=cc[:, 1:2]),
            reads=["vec", "cc"], writes=["tmp8"], tag=f"actwarm{wi}")
    pl.add("scalar", lambda eng, sem: eng.activation(
        tmp8[:, :], vec_sb[:, 0, :], AF.Exp, bias=cc[:, 1:2], scale=cc[:, 0:1]),
        reads=["vec", "cc", "tmp8"], writes=["tmp8"], tag="expw")
    pl.add("scalar", lambda eng, sem: eng.activation(
        a_sb[:, :], tmp8[:, :], AF.Exp, bias=cc[:, 1:2], scale=cc[:, 4:5]),
        reads=["tmp8", "cc"], writes=["a"], tag="expa")
    pl.add("scalar", lambda eng, sem: eng.activation(
        eu_sb[:, :], vec_sb[:, 1, :], AF.Exp, bias=cc[:, 1:2], scale=cc[:, 0:1]),
        reads=["vec", "cc"], writes=["eu"], tag="expu")
    pl.add("scalar", lambda eng, sem: eng.activation(
        bu_sb[:, :], vec_sb[:, 1, :], AF.Relu, bias=cc[:, 1:2], scale=cc[:, 0:1]),
        reads=["vec", "cc"], writes=["bu"], tag="relu_u")
    dma("sync", ones_tcs[:, :], ones_tc, writes=["onestc"], tag="z_ot")
    def em_afull(eng, sem):
        for j in range(NCCH):
            eng.activation(a_full[:, j, :], ones_tcs[:, :], AF.Identity,
                           bias=cc[:, 1:2],
                           scale=a_sb[:, j:j + 1]).then_inc(sem, 1)
        return None
    pl.add("scalar", em_afull, reads=["a", "onestc", "cc", "bu"],
           writes=["a_full"], inc=NCCH, tag="afull")
    dma("sync", m_run[:, :], zconst[:, 8:16], writes=["m_run"], tag="z_mr")
    dma("sync", p_car[:, :, 0], zconst[:, 0:8], writes=["p_car"], tag="z_pc")
    dma("sync", q_car[:, :, 0], zconst[:, 0:8], writes=["q_car"], tag="z_qc")
    dma("sync", cntacc[:, :], zconst[:, 0:8], writes=["cntacc"], tag="z_ct")
    dma("sync", lnprev[:, :, 0], zbf, writes=["lnprev"], tag="z_lp")

    def ln_stats(src_ap, reads, tag, si=0):
        vop(lambda eng, sem, s=src_ap: eng.bn_stats(stat6[:, si, 0:6], s[:, 0:512]),
            reads=reads, writes=[f"stat6a{si}"], tag=tag + "_bn0")
        vop(lambda eng, sem, s=src_ap: eng.bn_stats(stat6[:, si, 6:12], s[:, 512:1024]),
            reads=reads, writes=[f"stat6b{si}"], tag=tag + "_bn1")
        vop(lambda eng, sem: eng.bn_aggr(stat2[:, si, :], stat6[:, si, :]),
            reads=[f"stat6a{si}", f"stat6b{si}"], writes=[f"stat2{si}"],
            tag=tag + "_ag")
        pl.add("scalar", lambda eng, sem: eng.activation(
            lnv[:, si, :], stat2[:, si, 1:2], AF.Ln, bias=cc[:, 2:3],
            scale=cc[:, 0:1]),
            reads=[f"stat2{si}", "cc"], writes=[f"lnv{si}"], tag=tag + "_ln")
        pl.add("scalar", lambda eng, sem: eng.activation(
            rstd[:, si, :], lnv[:, si, :], AF.Exp, bias=cc[:, 1:2],
            scale=cc[:, 3:4]),
            reads=[f"lnv{si}", "cc"], writes=[f"rstd{si}"], tag=tag + "_ex")
        vop(lambda eng, sem: eng.scalar_tensor_tensor(
            nmr[:, si, :], stat2[:, si, 0:1], -1.0, rstd[:, si, :],
            ALU.mult, ALU.mult),
            reads=[f"stat2{si}", f"rstd{si}"], writes=[f"nmr{si}"],
            tag=tag + "_nm")

    def p1_front(c):
        t0 = c * TC
        par = c % 2
        tiles = [c * TPC + k for k in range(TPC)]
        dma_batch("sync",
                  [(xstage[:, par * 2 + k, :], xr_r[:, tiles[k], :])
                   for k in range(TPC)],
                  reads=[], writes=[f"xs{par * 2 + k}" for k in range(TPC)],
                  tag=f"xload{c}")
        for k in range(TPC):
            i = tiles[k]
            slot = par * 2 + k
            xs = xstage[:, slot, :]
            ln_stats(xs, [f"xs{slot}"], f"ln1_{i}", si=0)
            pl.add("scalar", lambda eng, sem, xs=xs, k=k: eng.activation(
                stage[:, k, :], xs, AF.Identity, bias=nmr[:, 0, :],
                scale=rstd[:, 0, :]),
                reads=["rstd0", "nmr0", f"xs{slot}"], writes=[f"stg{k}"],
                tag=f"ln1w_{i}")
            dma("sync", ln1h_r[:, i, :, :],
                stage[:, k, :].rearrange("p (j q) -> p j q", q=P),
                reads=[f"stg{k}"], writes=[f"l1h{c}"], tag=f"ln1st{i}")
        dma_batch("sync",
                  [(lnf[:, j, :], ln1h[j, t0:t0 + TC, :]) for j in range(NCCH)],
                  reads=[f"l1h{c}", "lnf"], writes=["lnf"], tag=f"l1tr{c}", tr=True)
        def em_d(eng, sem):
            eng.tensor_sub(dbuf[:, :, 1:TC], lnf[:, :, 1:TC],
                           lnf[:, :, 0:TC - 1]).then_inc(sem, 1)
            eng.tensor_sub(dbuf[:, :, 0:1], lnf[:, :, 0:1],
                           lnprev[:, :, :]).then_inc(sem, 1)
            return None
        pl.add("gpsimd", em_d, reads=["lnf", "lnprev"], writes=["dbuf"],
               inc=2, tag=f"mixd{c}")
        for dst, tokb, vi in ((xk_b, "xk", 2), (xv_b, "xv", 3), (xr_b, "xr", 4)):
            def em_mix(eng, sem, base, dst=dst, vi=vi, par=par):
                mkz = zs(vec_sb[:, vi, :], TC)
                eng.tensor_mul(dst[:, par, :, :], dbuf[:, :, :], mkz).then_inc(sem, 1)
                eng.wait_ge(sem, base + 1)
                eng.tensor_add(dst[:, par, :, 1:TC], dst[:, par, :, 1:TC],
                               lnf[:, :, 0:TC - 1]).then_inc(sem, 1)
                eng.tensor_add(dst[:, par, :, 0:1], dst[:, par, :, 0:1],
                               lnprev[:, :, :]).then_inc(sem, 1)
                return None
            em_mix._wants_base = True
            pl.add("gpsimd", em_mix, reads=["dbuf", "lnf", "lnprev", "vec",
                                            f"{tokb}{par}"],
                   writes=[f"{tokb}{par}"], inc=3, tag=f"mix{tokb}{c}")
        pl.add("gpsimd", lambda eng, sem: eng.tensor_copy(
            lnprev[:, :, :], lnf[:, :, TC - 1:TC]),
            reads=["lnf", f"xk{par}", f"xv{par}", f"xr{par}", "dbuf"],
            writes=["lnprev"], tag=f"lnprev{c}")

    def p1_back_mm(c):
        par = c % 2
        tiles = [c * TPC + k for k in range(TPC)]
        for wname, src, tokb in (("wkT", xk_b, "xk"), ("wvT", xv_b, "xv"),
                                 ("wrT", xr_b, "xr")):
            wsb = w_bf[wname]
            for dc in range(NCCH):
                bank = dc % 4
                psv = psA[:, bank, 0:TC]
                def em_mm(eng, sem, wsb=wsb, src=src, dc=dc, psv=psv, par=par):
                    for cj in range(NCCH):
                        ins = eng.matmul(psv, wsb[:, cj, dc * P:(dc + 1) * P],
                                         src[:, par, cj, :],
                                         start=(cj == 0), stop=(cj == NCCH - 1))
                    return ins
                pl.add("tensor", em_mm,
                       reads=[f"{tokb}{par}", "w_" + wname], writes=[f"BK{bank}"],
                       tag=f"mm_{wname}{c}_{dc}")
                if wname == "wkT":
                    vop(lambda eng, sem, dc=dc, psv=psv: eng.tensor_copy(
                        kf[:, dc, :], psv), reads=[f"BK{bank}"],
                        writes=["kf"], tag=f"evk{c}_{dc}")
                elif wname == "wvT":
                    pl.add("scalar", lambda eng, sem, dc=dc, psv=psv: eng.activation(
                        vf_b[:, dc, :], psv, AF.Identity, bias=cc[:, 1:2],
                        scale=cc[:, 0:1]),
                        reads=[f"BK{bank}", "cc"], writes=["vf"],
                        tag=f"evv{c}_{dc}")
                else:
                    pl.add("scalar", lambda eng, sem, dc=dc, psv=psv: eng.activation(
                        er[:, dc, :], psv, AF.Exp, bias=cc[:, 1:2],
                        scale=cc[:, 4:5]),
                        reads=[f"BK{bank}", "cc"],
                        writes=["er"], tag=f"evr{c}_{dc}")

        if _DEBUG_CANARY:
            for st, (buf3, tok) in enumerate((
                    (xk_b, f"xk{par}"), (xv_b, f"xv{par}"), (xr_b, f"xr{par}"))):
                vop(lambda eng, sem, st=st, buf3=buf3, par=par, c=c:
                    eng.reduce_max(canary[:, st, :, c], buf3[:, par, :, :],
                                   axis=AXX, apply_absolute_value=True),
                    reads=[tok], writes=["canary"], tag=f"cn{st}_{c}")
            vop(lambda eng, sem, c=c: eng.reduce_max(
                canary[:, 3, :, c], kf[:, :, :], axis=AXX,
                apply_absolute_value=True),
                reads=["kf"], writes=["canary"], tag=f"cn3_{c}")

    def p1_back_chain(c):
        par = c % 2
        vop(lambda eng, sem: eng.reduce_max(mloc[:, :], kf[:, :, :], axis=AXX),
            reads=["kf"], writes=["mloc"], tag=f"mloc{c}")
        vop(lambda eng, sem: eng.tensor_max(m_new[:, :], m_run[:, :], mloc[:, :]),
            reads=["m_run", "mloc"], writes=["m_new"], tag=f"mnew{c}")
        vop(lambda eng, sem: eng.tensor_sub(m_d[:, :], m_run[:, :], m_new[:, :]),
            reads=["m_run", "m_new"], writes=["m_d"], tag=f"md{c}")
        vop(lambda eng, sem: eng.tensor_copy(m_run[:, :], m_new[:, :]),
            reads=["m_new", "m_d"], writes=["m_run"], tag=f"mup{c}")
        pl.add("scalar", lambda eng, sem: eng.activation(
            m_r[:, :, 0], m_d[:, :], AF.Exp, bias=cc[:, 1:2], scale=cc[:, 0:1]),
            reads=["m_d", "cc"], writes=["m_r"], tag=f"mr{c}")
        vop(lambda eng, sem: eng.scalar_tensor_tensor(
            negmp[:, :], m_new[:, :], -1.0, bu_sb[:, :], ALU.mult, ALU.subtract),
            reads=["m_new", "bu"], writes=["negmp"], tag=f"negmp{c}")
        vop(lambda eng, sem: eng.tensor_mul(bv_b[:, :, 0:1], p_car[:, :, :],
                                            m_r[:, :, :]),
            reads=["p_car", "m_r", "bv"], writes=["bv"], tag=f"pini{c}")
        vop(lambda eng, sem: eng.tensor_mul(ek_b[:, :, 0:1], q_car[:, :, :],
                                            m_r[:, :, :]),
            reads=["q_car", "m_r", "ek"], writes=["ek"], tag=f"qini{c}")
        def em_ek(eng, sem):
            for dc in range(NCCH):
                eng.activation(ek_b[:, dc, 1:TC + 1], kf[:, dc, :], AF.Exp,
                               bias=negmp[:, dc:dc + 1],
                               scale=cc[:, 0:1]).then_inc(sem, 1)
            return None
        pl.add("scalar", em_ek, reads=["kf", "negmp", "cc"], writes=["ek"],
               inc=NCCH, tag=f"ek{c}")
        vop(lambda eng, sem: eng.tensor_mul(bv_b[:, :, 1:TC + 1],
                                            ek_b[:, :, 1:TC + 1],
                                            vf_b[:, :, :]),
            reads=["ek", "vf"], writes=["bv"], tag=f"bv{c}")
        if _DEBUG_CANARY:
            vop(lambda eng, sem, c=c: eng.reduce_max(
                canary[:, 4, :, c], ek_b[:, :, :], axis=AXX,
                apply_absolute_value=True),
                reads=["ek"], writes=["canary"], tag=f"cn4_{c}")
            vop(lambda eng, sem, c=c: eng.reduce_max(
                canary[:, 5, :, c], bv_b[:, :, :], axis=AXX,
                apply_absolute_value=True),
                reads=["bv"], writes=["canary"], tag=f"cn5_{c}")
            vop(lambda eng, sem, c=c: eng.reduce_max(
                canary[:, 6, :, c], er[:, :, :], axis=AXX,
                apply_absolute_value=True),
                reads=["er"], writes=["canary"], tag=f"cn6_{c}")
        def em_scanp(eng, sem):
            for dc in range(NCCH):
                eng.tensor_tensor_scan(
                    p_scan[:, dc, 0:TC + 1], a_full[:, dc, :], bv_b[:, dc, :],
                    0.0, ALU.mult, ALU.add).then_inc(sem, 1)
            return None
        pl.add("vector", em_scanp, reads=["a_full", "bv", "p_scan"],
               writes=["p_scan"], inc=NCCH, tag=f"scanp{c}")
        def em_scanq(eng, sem):
            for dc in range(NCCH):
                eng.tensor_tensor_scan(
                    q_scan[:, dc, 0:TC + 1], a_full[:, dc, :], ek_b[:, dc, :],
                    0.0, ALU.mult, ALU.add).then_inc(sem, 1)
            return None
        pl.add("vector", em_scanq, reads=["a_full", "ek", "q_scan"],
               writes=["q_scan"], inc=NCCH, tag=f"scanq{c}")
        if _DEBUG_CANARY:
            vop(lambda eng, sem, c=c: eng.reduce_max(
                canary[:, 7, :, c], p_scan[:, :, 1:TC + 1], axis=AXX,
                apply_absolute_value=True),
                reads=["p_scan"], writes=["canary"], tag=f"cn7_{c}")
        if _DEBUG_DUMPS:
            dma("sync", psd[c], p_scan[:, :, :], reads=["p_scan"],
                writes=[], tag=f"dpsd{c}")
            dma("sync", ekd[c], ek_b[:, :, :], reads=["ek"],
                writes=[], tag=f"dekd{c}")
            dma("sync", vfd[c], vf_b[:, :, :], reads=["vf"],
                writes=[], tag=f"dvfd{c}")
        vop(lambda eng, sem: eng.tensor_copy(p_car[:, :, :], p_scan[:, :, TC:TC + 1]),
            reads=["p_scan"], writes=["p_car"], tag=f"pcar{c}")
        vop(lambda eng, sem: eng.tensor_copy(q_car[:, :, :], q_scan[:, :, TC:TC + 1]),
            reads=["q_scan"], writes=["q_car"], tag=f"qcar{c}")
        def em_num(eng, sem, base):
            euz = zs(eu_sb[:, :], TC)
            eng.tensor_mul(num_b[:, :, :], bv_b[:, :, 1:TC + 1], euz).then_inc(sem, 1)
            eng.wait_ge(sem, base + 1)
            eng.tensor_add(num_b[:, :, :], num_b[:, :, :],
                           p_scan[:, :, 0:TC]).then_inc(sem, 1)
            return None
        em_num._wants_base = True
        pl.add("gpsimd", em_num, reads=["bv", "eu", "p_scan"], writes=["num"],
               inc=2, tag=f"num{c}")
        def em_den(eng, sem, base):
            euz = zs(eu_sb[:, :], TC)
            eng.tensor_mul(den_b[:, :, :], ek_b[:, :, 1:TC + 1], euz).then_inc(sem, 1)
            eng.wait_ge(sem, base + 1)
            eng.tensor_add(den_b[:, :, :], den_b[:, :, :],
                           q_scan[:, :, 0:TC]).then_inc(sem, 1)
            return None
        em_den._wants_base = True
        pl.add("gpsimd", em_den, reads=["ek", "eu", "q_scan"], writes=["den"],
               inc=2, tag=f"den{c}")
        pl.add("scalar", lambda eng, sem: eng.activation(
            kf[:, :, :], er[:, :, :], AF.Ln, bias=cc[:, 0:1], scale=cc[:, 0:1]),
            reads=["er", "cc", "ek"], writes=["kf"], tag=f"srl{c}")
        pl.add("scalar", lambda eng, sem: eng.activation(
            er[:, :, :], kf[:, :, :], AF.Exp, bias=cc[:, 1:2], scale=cc[:, 4:5]),
            reads=["kf", "cc"], writes=["er"], tag=f"sre{c}")
        pl.add("scalar", lambda eng, sem: eng.activation(
            kf[:, :, :], den_b[:, :, :], AF.Ln, bias=cc[:, 1:2], scale=cc[:, 0:1]),
            reads=["den", "cc", "er"], writes=["kf"], tag=f"rdl{c}")
        pl.add("scalar", lambda eng, sem: eng.activation(
            den_b[:, :, :], kf[:, :, :], AF.Exp, bias=cc[:, 1:2], scale=cc[:, 4:5]),
            reads=["kf", "cc"], writes=["den"], tag=f"rde{c}")
        vop(lambda eng, sem: eng.tensor_mul(num_b[:, :, :], num_b[:, :, :],
                                            den_b[:, :, :]),
            reads=["num", "den"], writes=["num"], tag=f"wkv{c}")
        vop(lambda eng, sem: eng.tensor_mul(rw_b[:, :, :], num_b[:, :, :],
                                            er[:, :, :]),
            reads=["num", "er"], writes=["rw"], tag=f"rwkv{c}")
        if _DEBUG_DUMPS:
            dma("sync", rwd[c], rw_b[:, :, :], reads=["rw"],
                writes=[], tag=f"drwd{c}")


    def p1_back_wo(c):
        par = c % 2
        tiles = [c * TPC + k for k in range(TPC)]
        for k in range(TPC):
            i = tiles[k]
            slot = par * 2 + k
            def em_wo(eng, sem, k=k):
                for cj in range(NCCH):
                    for hh in range(2):
                        ins = eng.matmul(
                            psA[:, 4 + hh, :],
                            rw_b[:, cj, k * P:(k + 1) * P],
                            wo_bf[:, cj, hh * 512:(hh + 1) * 512],
                            start=(cj == 0), stop=(cj == NCCH - 1))
                return ins
            pl.add("tensor", em_wo, reads=["rw", "w_woT"],
                   writes=["BK4", "BK5"], tag=f"wo{c}_{k}")
            vop(lambda eng, sem: eng.bn_stats(stat6[:, 1, 0:6], psA[:, 4, :]),
                reads=["BK4"], writes=["stat6a1"], tag=f"wob0{i}")
            vop(lambda eng, sem: eng.bn_stats(stat6[:, 1, 6:12], psA[:, 5, :]),
                reads=["BK5"], writes=["stat6b1"], tag=f"wob1{i}")
            vop(lambda eng, sem: eng.bn_aggr(stat2[:, 1, :], stat6[:, 1, :]),
                reads=["stat6a1", "stat6b1"], writes=["stat21"], tag=f"woag{i}")
            pl.add("scalar", lambda eng, sem: eng.activation(
                lnv[:, 1, :], stat2[:, 1, 1:2], AF.Ln, bias=cc[:, 2:3],
                scale=cc[:, 0:1]),
                reads=["stat21", "cc"], writes=["lnv1"], tag=f"woln{i}")
            pl.add("scalar", lambda eng, sem: eng.activation(
                rstd[:, 1, :], lnv[:, 1, :], AF.Exp, bias=cc[:, 1:2],
                scale=cc[:, 3:4]),
                reads=["lnv1", "cc"], writes=["rstd1"], tag=f"woex{i}")
            vop(lambda eng, sem: eng.scalar_tensor_tensor(
                nmr[:, 1, :], stat2[:, 1, 0:1], -1.0, rstd[:, 1, :],
                ALU.mult, ALU.mult),
                reads=["stat21", "rstd1"], writes=["nmr1"], tag=f"wonm{i}")
            pl.add("scalar", lambda eng, sem, k=k: eng.activation(
                ystage[:, k, :], psA[:, 4:6, :], AF.Identity, bias=nmr[:, 1, :],
                scale=rstd[:, 1, :]),
                reads=["rstd1", "nmr1", "BK4", "BK5"], writes=[f"y{k}"],
                tag=f"woy{i}")
            vop(lambda eng, sem, k=k, slot=slot: eng.tensor_add(
                ystage[:, k, :], ystage[:, k, :], xstage[:, slot, :]),
                reads=[f"y{k}", f"xs{slot}"], writes=[f"y{k}"], tag=f"wox{i}")
            dma("sync", out_r[:, i, :], ystage[:, k, :],
                reads=[f"y{k}"], writes=[f"oh{i}"], tag=f"x1st{i}")

    def precast(e):
        dma("gpsimd", ewh[e], ewT[e], reads=[], writes=[f"ewh{e}"],
            tag=f"ewcast{e}")

    p1_front(0)
    p1_front(1)
    p1_back_mm(0)
    for c in range(NCH):
        p1_back_chain(c)
        if c + 1 < NCH:
            p1_back_mm(c + 1)
        p1_back_wo(c)
        if c + 2 < NCH:
            p1_front(c + 2)
        precast(c)

    pl.barrier("ph_ffn")

    for c in range(NCH):
        t0 = c * TC
        tiles = [c * TPC + k for k in range(TPC)]
        dma_batch("sync",
                  [(xstage[:, (c % 2) * 2 + k, :], out_r[:, tiles[k], :])
                   for k in range(TPC)],
                  reads=[f"oh{t}" for t in tiles] + [f"xs{(c % 2) * 2 + k}" for k in range(TPC)],
                  writes=[f"xs{(c % 2) * 2 + k}" for k in range(TPC)],
                  tag=f"x2load{c}")
        for k in range(TPC):
            i = tiles[k]
            slot = (c % 2) * 2 + k
            xs = xstage[:, slot, :]
            ln_stats(xs, [f"xs{slot}"], f"ln2_{i}", si=0)
            pl.add("scalar", lambda eng, sem, xs=xs, k=k: eng.activation(
                stage[:, k, :], xs, AF.Identity, bias=nmr[:, 0, :],
                scale=rstd[:, 0, :]),
                reads=["rstd0", "nmr0", f"xs{slot}"], writes=[f"stg{k}"],
                tag=f"ln2w_{i}")
            dma("sync", ln2h_r[:, i, :, :],
                stage[:, k, :].rearrange("p (j q) -> p j q", q=P),
                reads=[f"stg{k}"], writes=[f"l2h{c}"], tag=f"ln2st{i}")
        dma_batch("sync",
                  [(ln2f[:, j, t0:t0 + TC], ln2h[j, t0:t0 + TC, :])
                   for j in range(NCCH)],
                  reads=[f"l2h{c}"], writes=[f"l2f{c}"], tag=f"l2tr{c}", tr=True)
        for k in range(TPC):
            i = tiles[k]
            gbk = 6 + (i % 2)
            def em_gmm(eng, sem, i=i, gbk=gbk):
                for cj in range(NCCH):
                    ins = eng.matmul(psA[:, gbk, 0:NE],
                                     ln2f[:, cj, i * P:(i + 1) * P],
                                     gw_b[:, cj, :],
                                     start=(cj == 0), stop=(cj == NCCH - 1))
                return ins
            pl.add("tensor", em_gmm, reads=[f"l2f{c}", "gwb"],
                   writes=[f"BK{gbk}"], tag=f"gmm{i}")
            vop(lambda eng, sem, gbk=gbk: eng.tensor_add(g_sb[:, :],
                                                psA[:, gbk, 0:NE],
                                                gb_sb[:, :]),
                reads=[f"BK{gbk}", "gb"], writes=["g"], tag=f"gadd{i}")
            vop(lambda eng, sem, i=i: eng.tensor_copy(gdump[:, i, :], g_sb[:, :]),
                reads=["g"], writes=["gdump"], tag=f"gdmp{i}")
            vop(lambda eng, sem: eng.max(top8[:, :], g_sb[:, :]),
                reads=["g"], writes=["top8"], tag=f"gtop{i}")
            vop(lambda eng, sem: eng.tensor_scalar(ind_sb[:, :], g_sb[:, :],
                                                   top8[:, 1:2], None, ALU.is_ge),
                reads=["g", "top8"], writes=["ind"], tag=f"gind{i}")
            vop(lambda eng, sem: eng.tensor_add(cntacc[:, :], cntacc[:, :],
                                                ind_sb[:, :]),
                reads=["cntacc", "ind"], writes=["cntacc"], tag=f"gcnt{i}")
    pl.add("tensor", lambda eng, sem: eng.matmul(
        psA[0:8, 5, 0:1], cntacc[:, :], cc[:, 5:6], start=True, stop=True),
        reads=["cntacc", "cc"], writes=["BK5"], tag="cntmm")
    vop(lambda eng, sem: eng.tensor_copy(cnt8[:, :], psA[0:8, 5, 0:1]),
        reads=["BK5"], writes=["cnt8"], tag="cntev")
    dma("gpsimd", cc_in.ap(), cnt8[:, :], reads=["cnt8"], writes=["ccio"],
        tag="ccout")
    pl.add("gpsimd", lambda eng, sem: eng.collective_compute(
        "AllReduce", ALU.add, replica_groups=[list(range(NCORES))],
        ins=[cc_in.ap()], outs=[cc_out.ap()]),
        reads=["ccio"], writes=["ccres"], tag="allred", sem="ccs")
    dma("gpsimd", cnt8[:, :], cc_out.ap(), reads=["ccres", "cnt8"],
        writes=["cnt8"], tag="ccin")
    pl.add("tensor", lambda eng, sem: eng.matmul(
        psA[0:1, 5, 8:16], cnt8[:, :], id8_sb[:, :], start=True, stop=True),
        reads=["cnt8", "id8"], writes=["BK5"], tag="cnttr")
    vop(lambda eng, sem: eng.tensor_scalar(sT_sb[:, :], psA[0:1, 5, 8:16],
                                           INV_TOTAL, None, ALU.mult),
        reads=["BK5"], writes=["sT"], tag="sT")
    pl.add("tensor", lambda eng, sem: eng.matmul(
        psA[:, 5, 16:24], ones_sb[:, :], sT_sb[:, :], start=True, stop=True),
        reads=["sT", "onesr"], writes=["BK5"], tag="sbc_mm")
    vop(lambda eng, sem: eng.tensor_copy(s_bc[:, :], psA[:, 5, 16:24]),
        reads=["BK5"], writes=["s_bc"], tag="sbc_ev")
    def em_sbias(eng, sem):
        for dc in range(NCCH):
            eng.tensor_mul(sbias[:, dc, :], eb_sb[:, dc, :],
                           s_bc[:, :]).then_inc(sem, 1)
        return None
    pl.add("vector", em_sbias, reads=["eb", "s_bc"], writes=["sbias"],
           inc=NCCH, tag="sbias")

    l2deps = [f"l2f{c}" for c in range(NCH)]
    for e_i in range(NE):
        for half in range(2):
            par = (2 * e_i + half) % 2
            dma("sync", ewt_bf[:, par, :, :],
                ewh[e_i][:, half * 512:(half + 1) * 512]
                .rearrange("(j p) d -> p j d", p=P),
                reads=[f"ew{par}", f"ewh{e_i}"], writes=[f"ew{par}"],
                tag=f"ewl{e_i}_{half}")
            for dq in range(4):
                dc = half * 4 + dq
                b0 = 4 * (dc % 2)
                def em_emm(eng, sem, par=par, dq=dq, b0=b0):
                    for cj in range(NCCH):
                        for tq in range(4):
                            ins = eng.matmul(
                                psA[:, b0 + tq, :],
                                ewt_bf[:, par, cj, dq * P:(dq + 1) * P],
                                ln2f[:, cj, tq * 512:(tq + 1) * 512],
                                start=(cj == 0), stop=(cj == NCCH - 1))
                    return ins
                bks = [f"BK{b0 + q}" for q in range(4)]
                pl.add("tensor", em_emm,
                       reads=l2deps + [f"ew{par}"],
                       writes=bks, tag=f"emm{e_i}_{dc}")
                for tq in range(4):
                    sl = (dc % 2) * 4 + tq
                    col = slice(tq * 512, (tq + 1) * 512)
                    if e_i == 0:
                        pl.add("scalar", lambda eng, sem, dc=dc, b0=b0, tq=tq, col=col:
                               eng.activation(acc_t[:, dc, col], psA[:, b0 + tq, :],
                                              AF.Relu, bias=sbias[:, dc, 0:1],
                                              scale=s_bc[:, 0:1]),
                               reads=[f"BK{b0 + tq}", "sbias", "s_bc"],
                               writes=[f"acc{dc}"], tag=f"eev{e_i}_{dc}_{tq}")
                    else:
                        pl.add("scalar", lambda eng, sem, dc=dc, b0=b0, tq=tq, e_i=e_i, sl=sl:
                               eng.activation(h_t[:, sl, :], psA[:, b0 + tq, :],
                                              AF.Relu, bias=sbias[:, dc, e_i:e_i + 1],
                                              scale=s_bc[:, e_i:e_i + 1]),
                               reads=[f"BK{b0 + tq}", "sbias", "s_bc", f"h{sl}"],
                               writes=[f"h{sl}"], tag=f"eev{e_i}_{dc}_{tq}")
                        addeng = "vector" if (dc % 2 == 0) else "gpsimd"
                        if e_i < NE - 1:
                            pl.add(addeng, lambda eng, sem, dc=dc, col=col, sl=sl:
                                   eng.tensor_add(acc_t[:, dc, col],
                                                  acc_t[:, dc, col], h_t[:, sl, :]),
                                   reads=[f"acc{dc}", f"h{sl}"],
                                   writes=[f"acc{dc}", f"h{sl}"],
                                   tag=f"eadd{e_i}_{dc}_{tq}")
                        else:
                            pl.add(addeng, lambda eng, sem, dc=dc, col=col, sl=sl:
                                   eng.tensor_add(accbf_t[:, dc, col],
                                                  acc_t[:, dc, col], h_t[:, sl, :]),
                                   reads=[f"acc{dc}", f"h{sl}"],
                                   writes=["accbf", f"h{sl}"],
                                   tag=f"eaddf{e_i}_{dc}_{tq}")

    dma_batch("sync",
              [(acch_r[:, j, :, :],
                accbf_t[:, j, :].rearrange("p (i q) -> p i q", q=P))
               for j in range(NCCH)],
              reads=["accbf"], writes=["acch"], tag="accst")

    for i in range(NT):
        sl = i % 2
        dma("sync", att_st[:, sl, :], acch[i, :, :],
            reads=["acch", f"at{sl}"], writes=[f"at{sl}"], tag=f"attr{i}", tr=True)
        dma("sync", xstage[:, sl, :], out_r[:, i, :],
            reads=[f"oh{i}", f"xs{sl}"], writes=[f"xs{sl}"], tag=f"x4l{i}")
        ln_stats(att_st[:, sl, :], [f"at{sl}"], f"ln4_{i}", si=0)
        pl.add("scalar", lambda eng, sem, sl=sl: eng.activation(
            ystage[:, sl, :], att_st[:, sl, :], AF.Identity, bias=nmr[:, 0, :],
            scale=rstd[:, 0, :]),
            reads=["rstd0", "nmr0", f"at{sl}", f"y4{sl}"], writes=[f"y4{sl}"],
            tag=f"y2w{i}")
        vop(lambda eng, sem, sl=sl: eng.tensor_add(
            ystage[:, sl, :], ystage[:, sl, :], xstage[:, sl, :]),
            reads=[f"y4{sl}", f"xs{sl}"], writes=[f"y4{sl}"], tag=f"y2x{i}")
        dma("sync", out_r[:, i, :], ystage[:, sl, :],
            reads=[f"y4{sl}"], writes=[f"oh{i}", "outfin"], tag=f"ost{i}")

    tensors = dict(kf=kf, er=er, num_b=num_b, den_b=den_b, p_scan=p_scan,
                   q_scan=q_scan, ek_b=ek_b, bv_b=bv_b, rw_b=rw_b, lnf=lnf,
                   xk_b=xk_b, xv_b=xv_b, xr_b=xr_b, ln2f=ln2f, acc_t=acc_t,
                   s_bc=s_bc, cnt8=cnt8, g_sb=g_sb, cntacc=cntacc, ind_sb=ind_sb, top8=top8, gdump=gdump, gw_f=gw_f, gw_b=gw_b, canary=canary,
                   accbf_t=accbf_t, vf_b=vf_b, a_full=a_full, a_sb=a_sb,
                   m_run=m_run, eu_sb=eu_sb)
    dram_taps = {"ln1h": ln1h, "ln2h": ln2h, "acch": acch}
    for name, shape, dtype, key in _DEBUG_TAPS:
        tdr = nc.dram_tensor("tap_" + name, shape, dtype, kind="ExternalOutput").ap()
        if key.startswith("dram:"):
            src_ap = dram_taps[key[5:]].ap()
        else:
            src = tensors[key]
            src_ap = src[(slice(None),) * len(src.shape)]
        dma("sync", tdr, src_ap, reads=["outfin"], writes=[], tag="tap_" + name)

    pl.add("sync", lambda eng, sem: eng.nop(), reads=["outfin"], writes=[],
           tag="final")

    pl.schedule()
    import contextlib
    with contextlib.ExitStack() as ctx:
        sems = {s: ctx.enter_context(nc.semaphore("sem_" + s))
                for s in SEMS}
        with nc.Block() as block:
            block.sync(lambda eng: pl.emit_engine("sync", eng, sems))
            block.tensor(lambda eng: pl.emit_engine("tensor", eng, sems))
            block.vector(lambda eng: pl.emit_engine("vector", eng, sems))
            block.scalar(lambda eng: pl.emit_engine("scalar", eng, sems))
            block.gpsimd(lambda eng: pl.emit_engine("gpsimd", eng, sems))
    return nc


def _host_layout(inputs):
    x = np.asarray(inputs["L0_attn_x"])
    base = {
        "wkT": np.ascontiguousarray(np.asarray(inputs["L0_attn_Wk"]).T),
        "wvT": np.ascontiguousarray(np.asarray(inputs["L0_attn_Wv"]).T),
        "wrT": np.ascontiguousarray(np.asarray(inputs["L0_attn_Wr"]).T),
        "woT": np.ascontiguousarray(np.asarray(inputs["L0_attn_Wo"]).T),
        "ewT": np.ascontiguousarray(
            np.asarray(inputs["L0_ffn_expert_w"]).transpose(0, 2, 1)),
        "gwT": np.ascontiguousarray(np.asarray(inputs["L0_ffn_gate_w"]).T),
        "vecs": np.ascontiguousarray(np.stack([
            np.asarray(inputs["L0_attn_w"]).reshape(NCCH, P).T,
            np.asarray(inputs["L0_attn_u"]).reshape(NCCH, P).T,
            np.asarray(inputs["L0_attn_mix_k"]).reshape(NCCH, P).T,
            np.asarray(inputs["L0_attn_mix_v"]).reshape(NCCH, P).T,
            np.asarray(inputs["L0_attn_mix_r"]).reshape(NCCH, P).T,
        ])),
        "ebF": np.ascontiguousarray(
            np.asarray(inputs["L0_ffn_expert_b"]).reshape(NE, NCCH, P)
            .transpose(2, 1, 0)),
        "gbB": np.ascontiguousarray(np.broadcast_to(
            np.asarray(inputs["L0_ffn_gate_b"])[None, :], (P, NE))),
        "ccol": np.ascontiguousarray(np.broadcast_to(np.array(
            [1.0, 0.0, EPS, -0.5, -1.0, 1.0, 0.0, 0.0], np.float32)[None, :],
            (P, 8))),
        "id8": np.eye(8, dtype=np.float32),
        "ones_row": np.ones((1, P), np.float32),
        "zconst": np.concatenate([np.zeros((P, 8), np.float32),
                                  np.full((P, 8), -1e30, np.float32)], axis=1),
        "ones_tc": np.ones((P, TC + 1), np.float32),
    }
    base = {k: np.ascontiguousarray(v, dtype=np.float32) for k, v in base.items()}
    base["zbf"] = np.zeros((P, 8), ml_dtypes.bfloat16)
    maps = []
    for core in range(NCORES):
        m = dict(base)
        m["x"] = np.ascontiguousarray(x[core], dtype=np.float32)
        maps.append(m)
    return maps


def kernel(**inputs):
    maps = _host_layout(inputs)
    nc = build_nc()
    res = run_bass_kernel_spmd(nc, maps, core_ids=list(range(NCORES)))
    kernel.raw_results = res.results
    outp = np.stack([np.asarray(res.results[c]["out"]) for c in range(NCORES)])
    if _DEBUG_TAPS:
        kernel.tap_results = [
            {("tap_" + n): np.asarray(res.results[c]["tap_" + n])
             for n, _, _, _ in _DEBUG_TAPS} for c in range(NCORES)]
    return outp.astype(np.float32)


# version 14
# speedup vs baseline: 1.1153x; 1.0391x over previous
import numpy as np
import ml_dtypes
import concourse.bass as bass
import concourse.mybir as mybir
from concourse.bass_utils import run_bass_kernel_spmd

F32 = mybir.dt.float32
BF16 = mybir.dt.bfloat16
AF = mybir.ActivationFunctionType
ALU = mybir.AluOpType
AXX = mybir.AxisListType.X

P = 128
T = 2048
C = 1024
NCCH = 8
NT = 16
NCORES = 8
TC = 256
NCH = T // TC
TPC = TC // P
NE = 8
EPS = 1e-5
INV_TOTAL = 1.0 / 32768.0

ENGINES = ("sync", "tensor", "vector", "scalar", "gpsimd")
NSYNCP = 12
NGPSP = 6
SEMS = list(ENGINES) + [f"sd{i}" for i in range(NSYNCP)] + \
       [f"gd{i}" for i in range(NGPSP)] + ["ccs"]

_DEBUG_TAPS = []
_DEBUG_CANARY = False
_DEBUG_DUMPS = False


class _Op:
    __slots__ = ("engine", "emit", "reads", "writes", "inc", "waits", "tag",
                 "base", "sem")

    def __init__(self, engine, emit, reads, writes, inc, tag, sem):
        self.engine = engine
        self.emit = emit
        self.reads = reads
        self.writes = writes
        self.inc = inc
        self.waits = None
        self.tag = tag
        self.base = 0
        self.sem = sem


class Plan:
    def __init__(self):
        self.ops = []
        self.barrier_tok = None

    def add(self, engine, emit, reads=(), writes=(), inc=1, tag="", sem=None):
        r = list(reads)
        if self.barrier_tok is not None:
            r.append(self.barrier_tok)
        self.ops.append(_Op(engine, emit, r, list(writes), inc, tag,
                            sem or engine))

    def barrier(self, tok):
        bt = self.barrier_tok
        for en in ENGINES:
            self.add(en, lambda eng, sem: eng.nop(), reads=([bt] if bt else []),
                     writes=[tok], tag="barrier")
        self.barrier_tok = tok

    def schedule(self):
        last_writer = {}
        readers = {}
        cum = {s: 0 for s in SEMS}
        for op in self.ops:
            deps = {}
            for r in op.reads:
                w = last_writer.get(r)
                if w is not None:
                    deps[w[0]] = max(deps.get(w[0], 0), w[1])
            for w_ in op.writes:
                w = last_writer.get(w_)
                if w is not None:
                    deps[w[0]] = max(deps.get(w[0], 0), w[1])
                for sm, cval in readers.get(w_, ()):
                    deps[sm] = max(deps.get(sm, 0), cval)
            op.base = cum[op.sem]
            if op.sem != op.engine and op.base > 0:
                deps[op.sem] = max(deps.get(op.sem, 0), op.base)
            cum[op.sem] += op.inc
            my = (op.sem, cum[op.sem])
            for r in op.reads:
                readers.setdefault(r, []).append(my)
            for w_ in op.writes:
                last_writer[w_] = my
                readers[w_] = []
            op.waits = deps
        self.final = dict(cum)

    def emit_engine(self, engine_name, eng, sems):
        watermark = {s: 0 for s in SEMS}
        for op in self.ops:
            if op.engine != engine_name:
                continue
            for sm in sorted(op.waits):
                c = op.waits[sm]
                if c > watermark[sm]:
                    eng.wait_ge(sems[sm], c)
                    watermark[sm] = c
            if getattr(op.emit, "_wants_base", False):
                ins = op.emit(eng, sems[op.sem], op.base)
            else:
                ins = op.emit(eng, sems[op.sem])
            if ins is not None:
                ins.then_inc(sems[op.sem], op.inc)


def build_nc():
    nc = bass.Bass("TRN2", target_bir_lowering=False)
    pl = Plan()

    xin = nc.dram_tensor("x", [T, C], F32, kind="ExternalInput").ap()
    wT = {n: nc.dram_tensor(n, [C, C], F32, kind="ExternalInput").ap()
          for n in ("wkT", "wvT", "wrT", "woT")}
    ewT = nc.dram_tensor("ewT", [NE, C, C], F32, kind="ExternalInput").ap()
    gwT = nc.dram_tensor("gwT", [C, NE], F32, kind="ExternalInput").ap()
    vecs = nc.dram_tensor("vecs", [5, P, NCCH], F32, kind="ExternalInput").ap()
    ebF = nc.dram_tensor("ebF", [P, NCCH, NE], F32, kind="ExternalInput").ap()
    gbB = nc.dram_tensor("gbB", [P, NE], F32, kind="ExternalInput").ap()
    ccol = nc.dram_tensor("ccol", [P, 8], F32, kind="ExternalInput").ap()
    id8 = nc.dram_tensor("id8", [8, 8], F32, kind="ExternalInput").ap()
    ones_row = nc.dram_tensor("ones_row", [1, P], F32, kind="ExternalInput").ap()
    zconst = nc.dram_tensor("zconst", [P, 16], F32, kind="ExternalInput").ap()
    ones_tc = nc.dram_tensor("ones_tc", [P, TC + 1], F32, kind="ExternalInput").ap()
    zbf = nc.dram_tensor("zbf", [P, 8], BF16, kind="ExternalInput").ap()
    out = nc.dram_tensor("out", [T, C], F32, kind="ExternalOutput").ap()

    if _DEBUG_DUMPS:
        ekd = nc.dram_tensor("tap_ekd", [NCH, P, NCCH, TC + 1], BF16,
                             kind="ExternalOutput")
        rwd = nc.dram_tensor("tap_rwd", [NCH, P, NCCH, TC], BF16,
                             kind="ExternalOutput")
        psd = nc.dram_tensor("tap_psd", [NCH, P, NCCH, TC + 1], F32,
                             kind="ExternalOutput")
        vfd = nc.dram_tensor("tap_vfd", [NCH, P, NCCH, TC], BF16,
                             kind="ExternalOutput")
    ewh = nc.dram_tensor("ewh", [NE, C, C], BF16)
    ln1h = nc.dram_tensor("ln1h", [NCCH, T, P], BF16)
    ln2h = nc.dram_tensor("ln2h", [NCCH, T, P], BF16)
    acch = nc.dram_tensor("acch", [NT, C, P], BF16)
    cc_in = nc.dram_tensor("cc_in", [8, 1], F32)
    cc_out = nc.dram_tensor("cc_out", [8, 1], F32, addr_space="Shared")

    xr_r = xin.rearrange("(n p) c -> p n c", p=P)
    out_r = out.rearrange("(n p) c -> p n c", p=P)
    ln1h_r = ln1h.rearrange("j (n p) q -> p n j q", p=P)
    ln2h_r = ln2h.rearrange("j (n p) q -> p n j q", p=P)
    acch_r = acch.rearrange("i (j p) q -> p j i q", p=P)

    S = nc.alloc_sbuf_tensor
    vec_sb = S("vec_sb", [P, 5, NCCH], F32)
    eb_sb = S("eb_sb", [P, NCCH, NE], F32)
    gb_sb = S("gb_sb", [P, NE], F32)
    cc = S("cc", [P, 8], F32)
    id8_sb = S("id8_sb", [8, 8], F32)
    ones_sb = S("ones_sb", [1, P], F32)
    a_sb = S("a_sb", [P, NCCH], F32)
    eu_sb = S("eu_sb", [P, NCCH], F32)
    bu_sb = S("bu_sb", [P, NCCH], F32)
    tmp8 = S("tmp8", [P, NCCH], F32)
    m_run = S("m_run", [P, NCCH], F32)
    m_new = S("m_new", [P, NCCH], F32)
    m_d = S("m_d", [P, NCCH], F32)
    m_r = S("m_r", [P, NCCH, 1], F32)
    negmp = S("negmp", [P, NCCH], F32)
    mloc = S("mloc", [P, NCCH], F32)
    p_car = S("p_car", [P, NCCH, 1], F32)
    q_car = S("q_car", [P, NCCH, 1], F32)
    lnprev = S("lnprev", [P, NCCH, 1], BF16)
    stat6 = S("stat6", [P, 2, 12], F32)
    stat2 = S("stat2", [P, 2, 2], F32)
    lnv = S("lnv", [P, 2, 1], F32)
    rstd = S("rstd", [P, 2, 1], F32)
    nmr = S("nmr", [P, 2, 1], F32)
    gw_f = S("gw_f", [P, NCCH, NE], F32)
    gw_b = S("gw_b", [P, NCCH, NE], BF16)
    g_sb = S("g_sb", [P, NE], F32)
    top8 = S("top8", [P, NE], F32)
    ind_sb = S("ind_sb", [P, NE], F32)
    cntacc = S("cntacc", [P, NE], F32)
    cnt8 = S("cnt8", [8, 1], F32)
    sT_sb = S("sT_sb", [1, NE], F32)
    s_bc = S("s_bc", [P, NE], F32)
    sbias = S("sbias", [P, NCCH, NE], F32)
    if _DEBUG_TAPS or _DEBUG_CANARY:
        gdump = S("gdump", [P, NT, NE], F32)
        canary = S("canary", [P, 8, NCCH, NCH], F32)
    else:
        gdump = canary = None
    a_full = S("a_full", [P, NCCH, TC + 1], F32)
    ones_tcs = S("ones_tcs", [P, TC + 1], F32)
    xstage = S("xstage", [P, 4, C], F32)
    ystage = S("ystage", [P, 2, C], F32)
    stage = S("stage", [P, 2, C], BF16)
    att_st = S("att_st", [P, 2, C], BF16)

    wk_bf = S("wk_bf", [P, NCCH, C], BF16)
    arena0 = nc.lookup_mloc(wk_bf).addr
    wv_bf = S("wv_bf", [P, NCCH, C], BF16)
    wr_bf = S("wr_bf", [P, NCCH, C], BF16)
    wo_bf = S("wo_bf", [P, NCCH, C], BF16)
    w_bf = {"wkT": wk_bf, "wvT": wv_bf, "wrT": wr_bf, "woT": wo_bf}
    lnf = S("lnf", [P, NCCH, TC], BF16)
    dbuf = S("dbuf", [P, NCCH, TC], BF16)
    xk_b = S("xk_b", [P, 2, NCCH, TC], BF16)
    xv_b = S("xv_b", [P, 2, NCCH, TC], BF16)
    xr_b = S("xr_b", [P, 2, NCCH, TC], BF16)
    ek_b = S("ek_b", [P, NCCH, TC + 1], BF16)
    scr = S("scr", [P, NCCH, TC], F32)
    vf_b = S("vf_b", [P, 2, NCCH, TC], BF16)
    bv_b = S("bv_b", [P, NCCH, TC + 1], BF16)
    rw_b = S("rw_b", [P, NCCH, TC], BF16)
    kf = S("kf", [P, 2, NCCH, TC], F32)
    er = S("er", [P, 2, NCCH, TC], BF16)
    num_b = S("num_b", [P, NCCH, TC], BF16)
    den_b = S("den_b", [P, NCCH, TC], BF16)
    p_scan = S("p_scan", [P, NCCH, TC + 1], BF16)
    q_scan = S("q_scan", [P, NCCH, TC + 1], BF16)
    arena_end = nc.lookup_mloc(q_scan).addr + NCCH * (TC + 1) * 4

    AT = nc.alloc_sbuf_tensor_at
    off = arena0
    acc_t = AT("acc_t", [P, NCCH, T], F32, offset=off); off += NCCH * T * 4
    accbf_t = AT("accbf_t", [P, NCCH, T], BF16, offset=off); off += NCCH * T * 2
    ln2f = AT("ln2f", [P, NCCH, T], BF16, offset=off); off += NCCH * T * 2
    ewt_bf = AT("ewt_bf", [P, 2, NCCH, 512], BF16, offset=off); off += 2 * NCCH * 512 * 2
    h_t = AT("h_t", [P, 8, 512], F32, offset=off); off += 8 * 512 * 4
    assert off <= nc.sbuf_top, (off, arena_end, nc.sbuf_top)

    psA = nc.alloc_psum_tensor("psA", [P, 8, 512], F32)

    def zs(ap_obj, L):
        return bass.AP(ap_obj.tensor, ap_obj.offset, list(ap_obj.ap) + [[0, L]])

    dma_rr = {"sync": [0], "gpsimd": [0]}

    def _next_sem(engine):
        if engine == "sync":
            i = dma_rr["sync"][0] % NSYNCP
            dma_rr["sync"][0] += 1
            return f"sd{i}"
        else:
            i = dma_rr["gpsimd"][0] % NGPSP
            dma_rr["gpsimd"][0] += 1
            return f"gd{i}"

    def dma(engine, outap, inap, n=1, reads=(), writes=(), tag="dma", tr=False):
        if tr:
            def em(eng, sem, o=outap, i=inap):
                return eng.dma_start_transpose(o, i)
        else:
            def em(eng, sem, o=outap, i=inap):
                return eng.dma_start(out=o, in_=i)
        pl.add(engine, em, reads=reads, writes=writes, inc=16, tag=tag,
               sem=_next_sem(engine))

    def dma_batch(engine, pairs, reads=(), writes=(), tag="dmab", tr=False):
        def em(eng, sem, pairs=pairs, tr=tr):
            for o, i in pairs:
                if tr:
                    eng.dma_start_transpose(o, i).then_inc(sem, 16)
                else:
                    eng.dma_start(out=o, in_=i).then_inc(sem, 16)
            return None
        pl.add(engine, em, reads=reads, writes=writes, inc=16 * len(pairs),
               tag=tag, sem=_next_sem(engine))

    def vop(emit, reads=(), writes=(), tag="v", engine="vector"):
        pl.add(engine, emit, reads=reads, writes=writes, inc=1, tag=tag)

    dma("sync", vec_sb[:, :, :], vecs.rearrange("v p n -> p v n"), writes=["vec"])
    dma("sync", eb_sb[:, :, :], ebF, writes=["eb"])
    dma("sync", gb_sb[:, :], gbB, writes=["gb"])
    dma("sync", cc[:, :], ccol, writes=["cc"])
    dma("sync", id8_sb[:, :], id8, writes=["id8"])
    dma("sync", ones_sb[:, :], ones_row, writes=["onesr"])
    dma("gpsimd", gw_b[:, :, :], gwT.rearrange("(j p) e -> p j e", p=P),
        writes=["gwb"], tag="gwcast")
    for n in ("wkT", "wvT", "wrT", "woT"):
        dma("gpsimd", w_bf[n][:, :, :], wT[n].rearrange("(j p) d -> p j d", p=P),
            writes=["w_" + n], tag="wcast")
    for wi, fn in enumerate((AF.Exp, AF.Ln, AF.Relu, AF.Identity, AF.Exp)):
        pl.add("scalar", lambda eng, sem, fn=fn: eng.activation(
            tmp8[:, :], cc[:, 0:1].broadcast_to([P, NCCH]) if False else vec_sb[:, 0, :],
            fn, bias=cc[:, 0:1], scale=cc[:, 1:2]),
            reads=["vec", "cc"], writes=["tmp8"], tag=f"actwarm{wi}")
    pl.add("scalar", lambda eng, sem: eng.activation(
        tmp8[:, :], vec_sb[:, 0, :], AF.Exp, bias=cc[:, 1:2], scale=cc[:, 0:1]),
        reads=["vec", "cc", "tmp8"], writes=["tmp8"], tag="expw")
    pl.add("scalar", lambda eng, sem: eng.activation(
        a_sb[:, :], tmp8[:, :], AF.Exp, bias=cc[:, 1:2], scale=cc[:, 4:5]),
        reads=["tmp8", "cc"], writes=["a"], tag="expa")
    pl.add("scalar", lambda eng, sem: eng.activation(
        eu_sb[:, :], vec_sb[:, 1, :], AF.Exp, bias=cc[:, 1:2], scale=cc[:, 0:1]),
        reads=["vec", "cc"], writes=["eu"], tag="expu")
    pl.add("scalar", lambda eng, sem: eng.activation(
        bu_sb[:, :], vec_sb[:, 1, :], AF.Relu, bias=cc[:, 1:2], scale=cc[:, 0:1]),
        reads=["vec", "cc"], writes=["bu"], tag="relu_u")
    dma("sync", ones_tcs[:, :], ones_tc, writes=["onestc"], tag="z_ot")
    def em_afull(eng, sem):
        for j in range(NCCH):
            eng.activation(a_full[:, j, :], ones_tcs[:, :], AF.Identity,
                           bias=cc[:, 1:2],
                           scale=a_sb[:, j:j + 1]).then_inc(sem, 1)
        return None
    pl.add("scalar", em_afull, reads=["a", "onestc", "cc", "bu"],
           writes=["a_full"], inc=NCCH, tag="afull")
    dma("sync", m_run[:, :], zconst[:, 8:16], writes=["m_run"], tag="z_mr")
    dma("sync", p_car[:, :, 0], zconst[:, 0:8], writes=["p_car"], tag="z_pc")
    dma("sync", q_car[:, :, 0], zconst[:, 0:8], writes=["q_car"], tag="z_qc")
    dma("sync", cntacc[:, :], zconst[:, 0:8], writes=["cntacc"], tag="z_ct")
    dma("sync", lnprev[:, :, 0], zbf, writes=["lnprev"], tag="z_lp")

    def ln_stats(src_ap, reads, tag, si=0):
        vop(lambda eng, sem, s=src_ap: eng.bn_stats(stat6[:, si, 0:6], s[:, 0:512]),
            reads=reads, writes=[f"stat6a{si}"], tag=tag + "_bn0")
        vop(lambda eng, sem, s=src_ap: eng.bn_stats(stat6[:, si, 6:12], s[:, 512:1024]),
            reads=reads, writes=[f"stat6b{si}"], tag=tag + "_bn1")
        vop(lambda eng, sem: eng.bn_aggr(stat2[:, si, :], stat6[:, si, :]),
            reads=[f"stat6a{si}", f"stat6b{si}"], writes=[f"stat2{si}"],
            tag=tag + "_ag")
        pl.add("scalar", lambda eng, sem: eng.activation(
            lnv[:, si, :], stat2[:, si, 1:2], AF.Ln, bias=cc[:, 2:3],
            scale=cc[:, 0:1]),
            reads=[f"stat2{si}", "cc"], writes=[f"lnv{si}"], tag=tag + "_ln")
        pl.add("scalar", lambda eng, sem: eng.activation(
            rstd[:, si, :], lnv[:, si, :], AF.Exp, bias=cc[:, 1:2],
            scale=cc[:, 3:4]),
            reads=[f"lnv{si}", "cc"], writes=[f"rstd{si}"], tag=tag + "_ex")
        vop(lambda eng, sem: eng.scalar_tensor_tensor(
            nmr[:, si, :], stat2[:, si, 0:1], -1.0, rstd[:, si, :],
            ALU.mult, ALU.mult),
            reads=[f"stat2{si}", f"rstd{si}"], writes=[f"nmr{si}"],
            tag=tag + "_nm")

    def p1_front_a(c):
        t0 = c * TC
        par = c % 2
        tiles = [c * TPC + k for k in range(TPC)]
        dma_batch("sync",
                  [(xstage[:, par * 2 + k, :], xr_r[:, tiles[k], :])
                   for k in range(TPC)],
                  reads=[], writes=[f"xs{par * 2 + k}" for k in range(TPC)],
                  tag=f"xload{c}")
        for k in range(TPC):
            i = tiles[k]
            slot = par * 2 + k
            xs = xstage[:, slot, :]
            ln_stats(xs, [f"xs{slot}"], f"ln1_{i}", si=0)
            pl.add("scalar", lambda eng, sem, xs=xs, k=k: eng.activation(
                stage[:, k, :], xs, AF.Identity, bias=nmr[:, 0, :],
                scale=rstd[:, 0, :]),
                reads=["rstd0", "nmr0", f"xs{slot}"], writes=[f"stg{k}"],
                tag=f"ln1w_{i}")
            dma("sync", ln1h_r[:, i, :, :],
                stage[:, k, :].rearrange("p (j q) -> p j q", q=P),
                reads=[f"stg{k}"], writes=[f"l1h{c}"], tag=f"ln1st{i}")
        dma_batch("sync",
                  [(lnf[:, j, :], ln1h[j, t0:t0 + TC, :]) for j in range(NCCH)],
                  reads=[f"l1h{c}", "lnf"], writes=["lnf"], tag=f"l1tr{c}", tr=True)

    def p1_front_b(c):
        par = c % 2
        def em_d(eng, sem):
            eng.tensor_sub(dbuf[:, :, 1:TC], lnf[:, :, 1:TC],
                           lnf[:, :, 0:TC - 1]).then_inc(sem, 1)
            eng.tensor_sub(dbuf[:, :, 0:1], lnf[:, :, 0:1],
                           lnprev[:, :, :]).then_inc(sem, 1)
            return None
        pl.add("gpsimd", em_d, reads=["lnf", "lnprev"], writes=["dbuf"],
               inc=2, tag=f"mixd{c}")
        for dst, tokb, vi in ((xk_b, "xk", 2), (xv_b, "xv", 3), (xr_b, "xr", 4)):
            def em_mix(eng, sem, base, dst=dst, vi=vi, par=par):
                mkz = zs(vec_sb[:, vi, :], TC)
                eng.tensor_mul(dst[:, par, :, :], dbuf[:, :, :], mkz).then_inc(sem, 1)
                eng.wait_ge(sem, base + 1)
                eng.tensor_add(dst[:, par, :, 1:TC], dst[:, par, :, 1:TC],
                               lnf[:, :, 0:TC - 1]).then_inc(sem, 1)
                eng.tensor_add(dst[:, par, :, 0:1], dst[:, par, :, 0:1],
                               lnprev[:, :, :]).then_inc(sem, 1)
                return None
            em_mix._wants_base = True
            pl.add("gpsimd", em_mix, reads=["dbuf", "lnf", "lnprev", "vec",
                                            f"{tokb}{par}"],
                   writes=[f"{tokb}{par}"], inc=3, tag=f"mix{tokb}{c}")
        pl.add("gpsimd", lambda eng, sem: eng.tensor_copy(
            lnprev[:, :, :], lnf[:, :, TC - 1:TC]),
            reads=["lnf", f"xk{par}", f"xv{par}", f"xr{par}", "dbuf"],
            writes=["lnprev"], tag=f"lnprev{c}")

    def p1_back_mm(c):
        par = c % 2
        tiles = [c * TPC + k for k in range(TPC)]
        for wname, src, tokb in (("wkT", xk_b, "xk"), ("wvT", xv_b, "xv"),
                                 ("wrT", xr_b, "xr")):
            wsb = w_bf[wname]
            for dc in range(NCCH):
                bank = dc % 4
                psv = psA[:, bank, 0:TC]
                def em_mm(eng, sem, wsb=wsb, src=src, dc=dc, psv=psv, par=par):
                    for cj in range(NCCH):
                        ins = eng.matmul(psv, wsb[:, cj, dc * P:(dc + 1) * P],
                                         src[:, par, cj, :],
                                         start=(cj == 0), stop=(cj == NCCH - 1))
                    return ins
                pl.add("tensor", em_mm,
                       reads=[f"{tokb}{par}", "w_" + wname], writes=[f"BK{bank}"],
                       tag=f"mm_{wname}{c}_{dc}")
                if wname == "wkT":
                    vop(lambda eng, sem, dc=dc, psv=psv, par=par: eng.tensor_copy(
                        kf[:, par, dc, :], psv), reads=[f"BK{bank}"],
                        writes=[f"kf{par}"], tag=f"evk{c}_{dc}")
                elif wname == "wvT":
                    pl.add("scalar", lambda eng, sem, dc=dc, psv=psv, par=par:
                        eng.activation(vf_b[:, par, dc, :], psv, AF.Identity,
                                       bias=cc[:, 1:2], scale=cc[:, 0:1]),
                        reads=[f"BK{bank}", "cc"], writes=[f"vf{par}"],
                        tag=f"evv{c}_{dc}")
                else:
                    pl.add("scalar", lambda eng, sem, dc=dc, psv=psv, par=par:
                        eng.activation(er[:, par, dc, :], psv, AF.Exp,
                                       bias=cc[:, 1:2], scale=cc[:, 4:5]),
                        reads=[f"BK{bank}", "cc"],
                        writes=[f"er{par}"], tag=f"evr{c}_{dc}")

        if _DEBUG_CANARY:
            for st, (buf3, tok) in enumerate((
                    (xk_b, f"xk{par}"), (xv_b, f"xv{par}"), (xr_b, f"xr{par}"))):
                vop(lambda eng, sem, st=st, buf3=buf3, par=par, c=c:
                    eng.reduce_max(canary[:, st, :, c], buf3[:, par, :, :],
                                   axis=AXX, apply_absolute_value=True),
                    reads=[tok], writes=["canary"], tag=f"cn{st}_{c}")
            vop(lambda eng, sem, c=c: eng.reduce_max(
                canary[:, 3, :, c], kf[:, :, :], axis=AXX,
                apply_absolute_value=True),
                reads=["kf"], writes=["canary"], tag=f"cn3_{c}")

    def p1_back_chain(c):
        par = c % 2
        vop(lambda eng, sem, par=par: eng.reduce_max(mloc[:, :], kf[:, par, :, :],
                                                     axis=AXX),
            reads=[f"kf{par}"], writes=["mloc"], tag=f"mloc{c}")
        vop(lambda eng, sem: eng.tensor_max(m_new[:, :], m_run[:, :], mloc[:, :]),
            reads=["m_run", "mloc"], writes=["m_new"], tag=f"mnew{c}")
        vop(lambda eng, sem: eng.tensor_sub(m_d[:, :], m_run[:, :], m_new[:, :]),
            reads=["m_run", "m_new"], writes=["m_d"], tag=f"md{c}")
        vop(lambda eng, sem: eng.tensor_copy(m_run[:, :], m_new[:, :]),
            reads=["m_new", "m_d"], writes=["m_run"], tag=f"mup{c}")
        pl.add("scalar", lambda eng, sem: eng.activation(
            m_r[:, :, 0], m_d[:, :], AF.Exp, bias=cc[:, 1:2], scale=cc[:, 0:1]),
            reads=["m_d", "cc"], writes=["m_r"], tag=f"mr{c}")
        vop(lambda eng, sem: eng.scalar_tensor_tensor(
            negmp[:, :], m_new[:, :], -1.0, bu_sb[:, :], ALU.mult, ALU.subtract),
            reads=["m_new", "bu"], writes=["negmp"], tag=f"negmp{c}")
        vop(lambda eng, sem: eng.tensor_mul(bv_b[:, :, 0:1], p_car[:, :, :],
                                            m_r[:, :, :]),
            reads=["p_car", "m_r", "bv"], writes=["bv"], tag=f"pini{c}")
        vop(lambda eng, sem: eng.tensor_mul(ek_b[:, :, 0:1], q_car[:, :, :],
                                            m_r[:, :, :]),
            reads=["q_car", "m_r", "ek"], writes=["ek"], tag=f"qini{c}")
        def em_ek(eng, sem, par=par):
            for dc in range(NCCH):
                eng.activation(ek_b[:, dc, 1:TC + 1], kf[:, par, dc, :], AF.Exp,
                               bias=negmp[:, dc:dc + 1],
                               scale=cc[:, 0:1]).then_inc(sem, 1)
            return None
        pl.add("scalar", em_ek, reads=[f"kf{par}", "negmp", "cc"], writes=["ek"],
               inc=NCCH, tag=f"ek{c}")
        vop(lambda eng, sem, par=par: eng.tensor_mul(bv_b[:, :, 1:TC + 1],
                                            ek_b[:, :, 1:TC + 1],
                                            vf_b[:, par, :, :]),
            reads=["ek", f"vf{par}"], writes=["bv"], tag=f"bv{c}")
        def em_scanq(eng, sem):
            for dc in range(NCCH):
                eng.tensor_tensor_scan(
                    q_scan[:, dc, 0:TC + 1], a_full[:, dc, :], ek_b[:, dc, :],
                    0.0, ALU.mult, ALU.add).then_inc(sem, 1)
            return None
        pl.add("vector", em_scanq, reads=["a_full", "ek", "q_scan"],
               writes=["q_scan"], inc=NCCH, tag=f"scanq{c}")
        vop(lambda eng, sem: eng.tensor_copy(q_car[:, :, :], q_scan[:, :, TC:TC + 1]),
            reads=["q_scan"], writes=["q_car"], tag=f"qcar{c}")
        def em_den(eng, sem, base):
            euz = zs(eu_sb[:, :], TC)
            eng.tensor_mul(den_b[:, :, :], ek_b[:, :, 1:TC + 1], euz).then_inc(sem, 1)
            eng.wait_ge(sem, base + 1)
            eng.tensor_add(den_b[:, :, :], den_b[:, :, :],
                           q_scan[:, :, 0:TC]).then_inc(sem, 1)
            return None
        em_den._wants_base = True
        pl.add("gpsimd", em_den, reads=["ek", "eu", "q_scan"], writes=["den"],
               inc=2, tag=f"den{c}")
        def em_scanp(eng, sem):
            for dc in range(NCCH):
                eng.tensor_tensor_scan(
                    p_scan[:, dc, 0:TC + 1], a_full[:, dc, :], bv_b[:, dc, :],
                    0.0, ALU.mult, ALU.add).then_inc(sem, 1)
            return None
        pl.add("vector", em_scanp, reads=["a_full", "bv", "p_scan"],
               writes=["p_scan"], inc=NCCH, tag=f"scanp{c}")
        vop(lambda eng, sem: eng.tensor_copy(p_car[:, :, :], p_scan[:, :, TC:TC + 1]),
            reads=["p_scan"], writes=["p_car"], tag=f"pcar{c}")
        def em_num(eng, sem, base):
            euz = zs(eu_sb[:, :], TC)
            eng.tensor_mul(num_b[:, :, :], bv_b[:, :, 1:TC + 1], euz).then_inc(sem, 1)
            eng.wait_ge(sem, base + 1)
            eng.tensor_add(num_b[:, :, :], num_b[:, :, :],
                           p_scan[:, :, 0:TC]).then_inc(sem, 1)
            return None
        em_num._wants_base = True
        pl.add("vector", em_num, reads=["bv", "eu", "p_scan"], writes=["num"],
               inc=2, tag=f"num{c}")
        pl.add("scalar", lambda eng, sem, par=par: eng.activation(
            scr[:, :, :], er[:, par, :, :], AF.Ln, bias=cc[:, 0:1],
            scale=cc[:, 0:1]),
            reads=[f"er{par}", "cc"], writes=["scr"], tag=f"srl{c}")
        pl.add("scalar", lambda eng, sem, par=par: eng.activation(
            er[:, par, :, :], scr[:, :, :], AF.Exp, bias=cc[:, 1:2],
            scale=cc[:, 4:5]),
            reads=["scr", "cc"], writes=[f"er{par}"], tag=f"sre{c}")
        pl.add("scalar", lambda eng, sem: eng.activation(
            scr[:, :, :], den_b[:, :, :], AF.Ln, bias=cc[:, 1:2],
            scale=cc[:, 0:1]),
            reads=["den", "cc", f"er{par}"], writes=["scr"], tag=f"rdl{c}")
        pl.add("scalar", lambda eng, sem: eng.activation(
            den_b[:, :, :], scr[:, :, :], AF.Exp, bias=cc[:, 1:2],
            scale=cc[:, 4:5]),
            reads=["scr", "cc"], writes=["den"], tag=f"rde{c}")
        vop(lambda eng, sem: eng.tensor_mul(num_b[:, :, :], num_b[:, :, :],
                                            den_b[:, :, :]),
            reads=["num", "den"], writes=["num"], tag=f"wkv{c}")
        vop(lambda eng, sem, par=par: eng.tensor_mul(rw_b[:, :, :], num_b[:, :, :],
                                            er[:, par, :, :]),
            reads=["num", f"er{par}"], writes=["rw"], tag=f"rwkv{c}")
        if _DEBUG_DUMPS:
            dma("sync", rwd[c], rw_b[:, :, :], reads=["rw"],
                writes=[], tag=f"drwd{c}")


    def p1_back_wo(c):
        par = c % 2
        tiles = [c * TPC + k for k in range(TPC)]
        for k in range(TPC):
            i = tiles[k]
            slot = par * 2 + k
            def em_wo(eng, sem, k=k):
                for cj in range(NCCH):
                    for hh in range(2):
                        ins = eng.matmul(
                            psA[:, 4 + hh, :],
                            rw_b[:, cj, k * P:(k + 1) * P],
                            wo_bf[:, cj, hh * 512:(hh + 1) * 512],
                            start=(cj == 0), stop=(cj == NCCH - 1))
                return ins
            pl.add("tensor", em_wo, reads=["rw", "w_woT"],
                   writes=["BK4", "BK5"], tag=f"wo{c}_{k}")
            vop(lambda eng, sem: eng.bn_stats(stat6[:, 1, 0:6], psA[:, 4, :]),
                reads=["BK4"], writes=["stat6a1"], tag=f"wob0{i}")
            vop(lambda eng, sem: eng.bn_stats(stat6[:, 1, 6:12], psA[:, 5, :]),
                reads=["BK5"], writes=["stat6b1"], tag=f"wob1{i}")
            vop(lambda eng, sem: eng.bn_aggr(stat2[:, 1, :], stat6[:, 1, :]),
                reads=["stat6a1", "stat6b1"], writes=["stat21"], tag=f"woag{i}")
            pl.add("scalar", lambda eng, sem: eng.activation(
                lnv[:, 1, :], stat2[:, 1, 1:2], AF.Ln, bias=cc[:, 2:3],
                scale=cc[:, 0:1]),
                reads=["stat21", "cc"], writes=["lnv1"], tag=f"woln{i}")
            pl.add("scalar", lambda eng, sem: eng.activation(
                rstd[:, 1, :], lnv[:, 1, :], AF.Exp, bias=cc[:, 1:2],
                scale=cc[:, 3:4]),
                reads=["lnv1", "cc"], writes=["rstd1"], tag=f"woex{i}")
            vop(lambda eng, sem: eng.scalar_tensor_tensor(
                nmr[:, 1, :], stat2[:, 1, 0:1], -1.0, rstd[:, 1, :],
                ALU.mult, ALU.mult),
                reads=["stat21", "rstd1"], writes=["nmr1"], tag=f"wonm{i}")
            pl.add("scalar", lambda eng, sem, k=k: eng.activation(
                ystage[:, k, :], psA[:, 4:6, :], AF.Identity, bias=nmr[:, 1, :],
                scale=rstd[:, 1, :]),
                reads=["rstd1", "nmr1", "BK4", "BK5"], writes=[f"y{k}"],
                tag=f"woy{i}")
            vop(lambda eng, sem, k=k, slot=slot: eng.tensor_add(
                ystage[:, k, :], ystage[:, k, :], xstage[:, slot, :]),
                reads=[f"y{k}", f"xs{slot}"], writes=[f"y{k}"], tag=f"wox{i}")
            dma("sync", out_r[:, i, :], ystage[:, k, :],
                reads=[f"y{k}"], writes=[f"oh{i}"], tag=f"x1st{i}")

    def precast(e):
        dma("gpsimd", ewh[e], ewT[e], reads=[], writes=[f"ewh{e}"],
            tag=f"ewcast{e}")

    p1_front_a(0)
    p1_front_b(0)
    p1_front_a(1)
    p1_back_mm(0)
    for c in range(NCH):
        if c + 1 < NCH:
            p1_front_b(c + 1)
            p1_back_mm(c + 1)
        p1_back_chain(c)
        p1_back_wo(c)
        if c + 2 < NCH:
            p1_front_a(c + 2)
        precast(c)

    pl.barrier("ph_ffn")

    for c in range(NCH):
        t0 = c * TC
        tiles = [c * TPC + k for k in range(TPC)]
        dma_batch("sync",
                  [(xstage[:, (c % 2) * 2 + k, :], out_r[:, tiles[k], :])
                   for k in range(TPC)],
                  reads=[f"oh{t}" for t in tiles] + [f"xs{(c % 2) * 2 + k}" for k in range(TPC)],
                  writes=[f"xs{(c % 2) * 2 + k}" for k in range(TPC)],
                  tag=f"x2load{c}")
        for k in range(TPC):
            i = tiles[k]
            slot = (c % 2) * 2 + k
            xs = xstage[:, slot, :]
            ln_stats(xs, [f"xs{slot}"], f"ln2_{i}", si=0)
            pl.add("scalar", lambda eng, sem, xs=xs, k=k: eng.activation(
                stage[:, k, :], xs, AF.Identity, bias=nmr[:, 0, :],
                scale=rstd[:, 0, :]),
                reads=["rstd0", "nmr0", f"xs{slot}"], writes=[f"stg{k}"],
                tag=f"ln2w_{i}")
            dma("sync", ln2h_r[:, i, :, :],
                stage[:, k, :].rearrange("p (j q) -> p j q", q=P),
                reads=[f"stg{k}"], writes=[f"l2h{c}"], tag=f"ln2st{i}")
        dma_batch("sync",
                  [(ln2f[:, j, t0:t0 + TC], ln2h[j, t0:t0 + TC, :])
                   for j in range(NCCH)],
                  reads=[f"l2h{c}"], writes=[f"l2f{c}"], tag=f"l2tr{c}", tr=True)
        for k in range(TPC):
            i = tiles[k]
            gbk = 6 + (i % 2)
            def em_gmm(eng, sem, i=i, gbk=gbk):
                for cj in range(NCCH):
                    ins = eng.matmul(psA[:, gbk, 0:NE],
                                     ln2f[:, cj, i * P:(i + 1) * P],
                                     gw_b[:, cj, :],
                                     start=(cj == 0), stop=(cj == NCCH - 1))
                return ins
            pl.add("tensor", em_gmm, reads=[f"l2f{c}", "gwb"],
                   writes=[f"BK{gbk}"], tag=f"gmm{i}")
            vop(lambda eng, sem, gbk=gbk: eng.tensor_add(g_sb[:, :],
                                                psA[:, gbk, 0:NE],
                                                gb_sb[:, :]),
                reads=[f"BK{gbk}", "gb"], writes=["g"], tag=f"gadd{i}")
            if gdump is not None:
                vop(lambda eng, sem, i=i: eng.tensor_copy(gdump[:, i, :],
                                                          g_sb[:, :]),
                    reads=["g"], writes=["gdump"], tag=f"gdmp{i}")
            vop(lambda eng, sem: eng.max(top8[:, :], g_sb[:, :]),
                reads=["g"], writes=["top8"], tag=f"gtop{i}")
            vop(lambda eng, sem: eng.tensor_scalar(ind_sb[:, :], g_sb[:, :],
                                                   top8[:, 1:2], None, ALU.is_ge),
                reads=["g", "top8"], writes=["ind"], tag=f"gind{i}")
            vop(lambda eng, sem: eng.tensor_add(cntacc[:, :], cntacc[:, :],
                                                ind_sb[:, :]),
                reads=["cntacc", "ind"], writes=["cntacc"], tag=f"gcnt{i}")
    pl.add("tensor", lambda eng, sem: eng.matmul(
        psA[0:8, 5, 0:1], cntacc[:, :], cc[:, 5:6], start=True, stop=True),
        reads=["cntacc", "cc"], writes=["BK5"], tag="cntmm")
    vop(lambda eng, sem: eng.tensor_copy(cnt8[:, :], psA[0:8, 5, 0:1]),
        reads=["BK5"], writes=["cnt8"], tag="cntev")
    dma("gpsimd", cc_in.ap(), cnt8[:, :], reads=["cnt8"], writes=["ccio"],
        tag="ccout")
    pl.add("gpsimd", lambda eng, sem: eng.collective_compute(
        "AllReduce", ALU.add, replica_groups=[list(range(NCORES))],
        ins=[cc_in.ap()], outs=[cc_out.ap()]),
        reads=["ccio"], writes=["ccres"], tag="allred", sem="ccs")
    dma("gpsimd", cnt8[:, :], cc_out.ap(), reads=["ccres", "cnt8"],
        writes=["cnt8"], tag="ccin")
    pl.add("tensor", lambda eng, sem: eng.matmul(
        psA[0:1, 5, 8:16], cnt8[:, :], id8_sb[:, :], start=True, stop=True),
        reads=["cnt8", "id8"], writes=["BK5"], tag="cnttr")
    vop(lambda eng, sem: eng.tensor_scalar(sT_sb[:, :], psA[0:1, 5, 8:16],
                                           INV_TOTAL, None, ALU.mult),
        reads=["BK5"], writes=["sT"], tag="sT")
    pl.add("tensor", lambda eng, sem: eng.matmul(
        psA[:, 5, 16:24], ones_sb[:, :], sT_sb[:, :], start=True, stop=True),
        reads=["sT", "onesr"], writes=["BK5"], tag="sbc_mm")
    vop(lambda eng, sem: eng.tensor_copy(s_bc[:, :], psA[:, 5, 16:24]),
        reads=["BK5"], writes=["s_bc"], tag="sbc_ev")
    def em_sbias(eng, sem):
        for dc in range(NCCH):
            eng.tensor_mul(sbias[:, dc, :], eb_sb[:, dc, :],
                           s_bc[:, :]).then_inc(sem, 1)
        return None
    pl.add("vector", em_sbias, reads=["eb", "s_bc"], writes=["sbias"],
           inc=NCCH, tag="sbias")

    l2deps = [f"l2f{c}" for c in range(NCH)]
    for e_i in range(NE):
        for half in range(2):
            par = (2 * e_i + half) % 2
            dma("sync", ewt_bf[:, par, :, :],
                ewh[e_i][:, half * 512:(half + 1) * 512]
                .rearrange("(j p) d -> p j d", p=P),
                reads=[f"ew{par}", f"ewh{e_i}"], writes=[f"ew{par}"],
                tag=f"ewl{e_i}_{half}")
            for dq in range(4):
                dc = half * 4 + dq
                b0 = 4 * (dc % 2)
                def em_emm(eng, sem, par=par, dq=dq, b0=b0):
                    for cj in range(NCCH):
                        for tq in range(4):
                            ins = eng.matmul(
                                psA[:, b0 + tq, :],
                                ewt_bf[:, par, cj, dq * P:(dq + 1) * P],
                                ln2f[:, cj, tq * 512:(tq + 1) * 512],
                                start=(cj == 0), stop=(cj == NCCH - 1))
                    return ins
                bks = [f"BK{b0 + q}" for q in range(4)]
                pl.add("tensor", em_emm,
                       reads=l2deps + [f"ew{par}"],
                       writes=bks, tag=f"emm{e_i}_{dc}")
                for tq in range(4):
                    sl = (dc % 2) * 4 + tq
                    col = slice(tq * 512, (tq + 1) * 512)
                    if e_i == 0:
                        pl.add("scalar", lambda eng, sem, dc=dc, b0=b0, tq=tq, col=col:
                               eng.activation(acc_t[:, dc, col], psA[:, b0 + tq, :],
                                              AF.Relu, bias=sbias[:, dc, 0:1],
                                              scale=s_bc[:, 0:1]),
                               reads=[f"BK{b0 + tq}", "sbias", "s_bc"],
                               writes=[f"acc{dc}"], tag=f"eev{e_i}_{dc}_{tq}")
                    else:
                        pl.add("scalar", lambda eng, sem, dc=dc, b0=b0, tq=tq, e_i=e_i, sl=sl:
                               eng.activation(h_t[:, sl, :], psA[:, b0 + tq, :],
                                              AF.Relu, bias=sbias[:, dc, e_i:e_i + 1],
                                              scale=s_bc[:, e_i:e_i + 1]),
                               reads=[f"BK{b0 + tq}", "sbias", "s_bc", f"h{sl}"],
                               writes=[f"h{sl}"], tag=f"eev{e_i}_{dc}_{tq}")
                        addeng = "vector" if (dc % 2 == 0) else "gpsimd"
                        if e_i < NE - 1:
                            pl.add(addeng, lambda eng, sem, dc=dc, col=col, sl=sl:
                                   eng.tensor_add(acc_t[:, dc, col],
                                                  acc_t[:, dc, col], h_t[:, sl, :]),
                                   reads=[f"acc{dc}", f"h{sl}"],
                                   writes=[f"acc{dc}", f"h{sl}"],
                                   tag=f"eadd{e_i}_{dc}_{tq}")
                        else:
                            pl.add(addeng, lambda eng, sem, dc=dc, col=col, sl=sl:
                                   eng.tensor_add(accbf_t[:, dc, col],
                                                  acc_t[:, dc, col], h_t[:, sl, :]),
                                   reads=[f"acc{dc}", f"h{sl}"],
                                   writes=["accbf", f"h{sl}"],
                                   tag=f"eaddf{e_i}_{dc}_{tq}")

    dma_batch("sync",
              [(acch_r[:, j, :, :],
                accbf_t[:, j, :].rearrange("p (i q) -> p i q", q=P))
               for j in range(NCCH)],
              reads=["accbf"], writes=["acch"], tag="accst")

    for i in range(NT):
        sl = i % 2
        dma("sync", att_st[:, sl, :], acch[i, :, :],
            reads=["acch", f"at{sl}"], writes=[f"at{sl}"], tag=f"attr{i}", tr=True)
        dma("sync", xstage[:, sl, :], out_r[:, i, :],
            reads=[f"oh{i}", f"xs{sl}"], writes=[f"xs{sl}"], tag=f"x4l{i}")
        ln_stats(att_st[:, sl, :], [f"at{sl}"], f"ln4_{i}", si=0)
        pl.add("scalar", lambda eng, sem, sl=sl: eng.activation(
            ystage[:, sl, :], att_st[:, sl, :], AF.Identity, bias=nmr[:, 0, :],
            scale=rstd[:, 0, :]),
            reads=["rstd0", "nmr0", f"at{sl}", f"y4{sl}"], writes=[f"y4{sl}"],
            tag=f"y2w{i}")
        vop(lambda eng, sem, sl=sl: eng.tensor_add(
            ystage[:, sl, :], ystage[:, sl, :], xstage[:, sl, :]),
            reads=[f"y4{sl}", f"xs{sl}"], writes=[f"y4{sl}"], tag=f"y2x{i}")
        dma("sync", out_r[:, i, :], ystage[:, sl, :],
            reads=[f"y4{sl}"], writes=[f"oh{i}", "outfin"], tag=f"ost{i}")

    tensors = dict(kf=kf, er=er, num_b=num_b, den_b=den_b, p_scan=p_scan,
                   q_scan=q_scan, ek_b=ek_b, bv_b=bv_b, rw_b=rw_b, lnf=lnf,
                   xk_b=xk_b, xv_b=xv_b, xr_b=xr_b, ln2f=ln2f, acc_t=acc_t,
                   s_bc=s_bc, cnt8=cnt8, g_sb=g_sb, cntacc=cntacc, ind_sb=ind_sb, top8=top8, gdump=gdump, gw_f=gw_f, gw_b=gw_b, canary=canary,
                   accbf_t=accbf_t, vf_b=vf_b, a_full=a_full, a_sb=a_sb,
                   m_run=m_run, eu_sb=eu_sb)
    dram_taps = {"ln1h": ln1h, "ln2h": ln2h, "acch": acch}
    for name, shape, dtype, key in _DEBUG_TAPS:
        tdr = nc.dram_tensor("tap_" + name, shape, dtype, kind="ExternalOutput").ap()
        if key.startswith("dram:"):
            src_ap = dram_taps[key[5:]].ap()
        else:
            src = tensors[key]
            src_ap = src[(slice(None),) * len(src.shape)]
        dma("sync", tdr, src_ap, reads=["outfin"], writes=[], tag="tap_" + name)

    pl.add("sync", lambda eng, sem: eng.nop(), reads=["outfin"], writes=[],
           tag="final")

    pl.schedule()
    import contextlib
    with contextlib.ExitStack() as ctx:
        sems = {s: ctx.enter_context(nc.semaphore("sem_" + s))
                for s in SEMS}
        with nc.Block() as block:
            block.sync(lambda eng: pl.emit_engine("sync", eng, sems))
            block.tensor(lambda eng: pl.emit_engine("tensor", eng, sems))
            block.vector(lambda eng: pl.emit_engine("vector", eng, sems))
            block.scalar(lambda eng: pl.emit_engine("scalar", eng, sems))
            block.gpsimd(lambda eng: pl.emit_engine("gpsimd", eng, sems))
    return nc


def _host_layout(inputs):
    x = np.asarray(inputs["L0_attn_x"])
    base = {
        "wkT": np.ascontiguousarray(np.asarray(inputs["L0_attn_Wk"]).T),
        "wvT": np.ascontiguousarray(np.asarray(inputs["L0_attn_Wv"]).T),
        "wrT": np.ascontiguousarray(np.asarray(inputs["L0_attn_Wr"]).T),
        "woT": np.ascontiguousarray(np.asarray(inputs["L0_attn_Wo"]).T),
        "ewT": np.ascontiguousarray(
            np.asarray(inputs["L0_ffn_expert_w"]).transpose(0, 2, 1)),
        "gwT": np.ascontiguousarray(np.asarray(inputs["L0_ffn_gate_w"]).T),
        "vecs": np.ascontiguousarray(np.stack([
            np.asarray(inputs["L0_attn_w"]).reshape(NCCH, P).T,
            np.asarray(inputs["L0_attn_u"]).reshape(NCCH, P).T,
            np.asarray(inputs["L0_attn_mix_k"]).reshape(NCCH, P).T,
            np.asarray(inputs["L0_attn_mix_v"]).reshape(NCCH, P).T,
            np.asarray(inputs["L0_attn_mix_r"]).reshape(NCCH, P).T,
        ])),
        "ebF": np.ascontiguousarray(
            np.asarray(inputs["L0_ffn_expert_b"]).reshape(NE, NCCH, P)
            .transpose(2, 1, 0)),
        "gbB": np.ascontiguousarray(np.broadcast_to(
            np.asarray(inputs["L0_ffn_gate_b"])[None, :], (P, NE))),
        "ccol": np.ascontiguousarray(np.broadcast_to(np.array(
            [1.0, 0.0, EPS, -0.5, -1.0, 1.0, 0.0, 0.0], np.float32)[None, :],
            (P, 8))),
        "id8": np.eye(8, dtype=np.float32),
        "ones_row": np.ones((1, P), np.float32),
        "zconst": np.concatenate([np.zeros((P, 8), np.float32),
                                  np.full((P, 8), -1e30, np.float32)], axis=1),
        "ones_tc": np.ones((P, TC + 1), np.float32),
    }
    base = {k: np.ascontiguousarray(v, dtype=np.float32) for k, v in base.items()}
    base["zbf"] = np.zeros((P, 8), ml_dtypes.bfloat16)
    maps = []
    for core in range(NCORES):
        m = dict(base)
        m["x"] = np.ascontiguousarray(x[core], dtype=np.float32)
        maps.append(m)
    return maps


def kernel(**inputs):
    maps = _host_layout(inputs)
    nc = build_nc()
    res = run_bass_kernel_spmd(nc, maps, core_ids=list(range(NCORES)))
    kernel.raw_results = res.results
    outp = np.stack([np.asarray(res.results[c]["out"]) for c in range(NCORES)])
    if _DEBUG_TAPS:
        kernel.tap_results = [
            {("tap_" + n): np.asarray(res.results[c]["tap_" + n])
             for n, _, _, _ in _DEBUG_TAPS} for c in range(NCORES)]
    return outp.astype(np.float32)
